# version 03
# baseline (speedup 1.0000x reference)
import numpy as np
import ml_dtypes
import concourse.bass as bass
import concourse.mybir as mybir
from concourse.bass_utils import run_bass_kernel_spmd
from contextlib import ExitStack

F32 = mybir.dt.float32
BF16 = mybir.dt.bfloat16
AF = mybir.ActivationFunctionType
ALU = mybir.AluOpType
AX = mybir.AxisListType

D = 1024
DI = 2048
DS = 16
E = 8
IN = 1024
SH = 2048
NCORES = 8

DEBUG = False


class Sched:
    ENGS = ("sync", "scalar", "vector", "tensor", "gpsimd")

    def __init__(self, nc):
        self.nc = nc
        self.items = {e: [] for e in self.ENGS}
        self.sems = {}
        self.cnt = {}

    def sem(self, name):
        self.sems[name] = self.nc.alloc_semaphore(name)
        self.cnt[name] = 0

    DMA_WINDOW = 8

    def op(self, eng, fn, waits=(), inc=None, incn=1):
        tok = None
        ws = [w for w in waits if w is not None]
        if inc is not None and incn == 16:
            gate = self.cnt[inc] - 16 * (self.DMA_WINDOW - 1)
            if gate > 0:
                ws.append((inc, gate))
        if inc is not None:
            self.cnt[inc] += incn
            tok = (inc, self.cnt[inc])
        self.items[eng].append((fn, tuple(ws), (inc, incn) if inc else None))
        return tok

    def snap(self):
        return [(s, v) for s, v in self.cnt.items() if v > 0]

    def emit(self, block):
        for e in self.ENGS:
            items = self.items[e]
            if not items:
                continue

            def body(h, items=items):
                lastw = {}
                for fn, ws, inc in items:
                    for (s, v) in ws:
                        if lastw.get(s, 0) < v:
                            h.wait_ge(self.sems[s], v)
                            lastw[s] = v
                    ins = fn(h)
                    if inc is not None:
                        s, n = inc
                        ins.then_inc(self.sems[s], n)
            getattr(block, e)(body)


def build_program(T):
    nc = bass.Bass("TRN2", target_bir_lowering=False)
    NT = T // 128
    NTC = T // 512
    HALF = T // 2
    HTC = HALF // 512
    NTH = NT // 2

    def din(name, shape, dt=F32):
        return nc.dram_tensor(name, list(shape), dt, kind="ExternalInput")

    x_h = din("x", (T, D))
    w_in_h = din("w_in", (D, 2 * DI))
    cw_h = din("cw", (128, 16 * 4))
    cb_h = din("cb", (128, 16))
    wx_h = din("w_x", (DI, 2 * DS))
    wdt_h = din("w_dt", (DI, DI))
    bdt_h = din("b_dt", (128, 16))
    wout_h = din("w_out", (DI, D))
    a_h = din("a_mat", (128, 16 * DS))
    dv_h = din("d_vec", (128, 16))
    gwt_h = din("gate_wt", (128, 8 * E), BF16)
    gb_h = din("gate_b", (1, E))
    w1t_h = din("w1t", (E * D, IN))
    b1t_h = din("b1t", (128, 8 * E))
    w3t_h = din("w3t", (E * D, IN))
    b3t_h = din("b3t", (128, 8 * E))
    w2t_h = din("w2t", (E * IN, D))
    b2_h = din("b2", (E, D))
    ws1t_h = din("ws1t", (D, SH))
    bs1_h = din("bs1", (128, 16))
    ws3t_h = din("ws3t", (D, SH))
    bs3_h = din("bs3", (128, 16))
    ws2t_h = din("ws2t", (SH, D))
    bs2_h = din("bs2", (128, 8))
    sel16_h = din("sel16", (16, 16 * 128), BF16)
    sel8_h = din("sel8", (8, 8 * 128))
    identb_h = din("identb", (128, 128), BF16)
    identf_h = din("identf", (128, 128))
    ones1_h = din("ones1", (1, 128))
    fc_h = din("fc", (128, 2))

    out_h = nc.dram_tensor("out", [T, D], F32, kind="ExternalOutput")

    sz_d = nc.dram_tensor("sz_d", [DI, T], BF16, kind="Internal")
    delta_d = nc.dram_tensor("delta_d", [DI, T], F32, kind="Internal")
    ybf_d = nc.dram_tensor("ybf_d", [DI, T], BF16, kind="Internal")
    x1_d = nc.dram_tensor("x1_d", [T, D], F32, kind="Internal")

    dbg = {}
    if DEBUG:
        for name, shape, dt in [
            ("dbg_u", (DI, T), BF16), ("dbg_delta", (DI, T), F32),
            ("dbg_bc", (16, 2 * T), BF16), ("dbg_x1", (T, D), F32),
            ("dbg_ybf", (DI, T), BF16),
            ("dbg_g", (T, E), F32),
        ]:
            dbg[name] = nc.dram_tensor(name, list(shape), dt, kind="ExternalOutput")

    S = Sched(nc)
    for s in ("dW", "dX", "dSp", "dO", "sT", "sV", "sA", "sP"):
        S.sem(s)

    with ExitStack() as st:
        ec = st.enter_context

        sel16 = ec(nc.sbuf_tensor("sel16_s", [16, 16 * 128], BF16))
        identb = ec(nc.sbuf_tensor("identb_s", [128, 128], BF16))
        identf = ec(nc.sbuf_tensor("identf_s", [128, 128], F32))
        ones1 = ec(nc.sbuf_tensor("ones1_s", [1, 128], F32))
        gbrow = ec(nc.sbuf_tensor("gbrow", [1, E], F32))
        gbbc = ec(nc.sbuf_tensor("gbbc", [128, E], F32))
        cw_t = ec(nc.sbuf_tensor("cw_t", [128, 16 * 4], F32))
        cb_t = ec(nc.sbuf_tensor("cb_t", [128, 16], F32))
        bdt_t = ec(nc.sbuf_tensor("bdt_t", [128, 16], F32))
        a_t = ec(nc.sbuf_tensor("a_t", [128, 16 * DS], F32))
        dv_t = ec(nc.sbuf_tensor("dv_t", [128, 16], F32))
        hc = ec(nc.sbuf_tensor("hc", [128, 16 * DS], F32))
        fc = ec(nc.sbuf_tensor("fc_s", [128, 2], F32))
        actfm = ec(nc.sbuf_tensor("actfm", [128, 8 * T], BF16))
        big = ec(nc.sbuf_tensor("big", [128, 32768], BF16))
        wa = ec(nc.sbuf_tensor("wa", [128, 16384], BF16))
        wb = ec(nc.sbuf_tensor("wb", [128, 16384], BF16))

        braw = actfm[0:16, 0:T]
        craw = actfm[0:16, T:2 * T]
        Bbc = wb[:, 0:8192]
        Cbc = wb[:, 8192:16384]
        ysb = wa[:, 0:16384].bitcast(F32)
        ybs = wb[:, 0:8192]
        ofm = wb[:, 8192:16384].bitcast(F32)
        he = wb[:, 8192:16384]
        gwt_t = big[:, 24576:24640]
        b1_t = big[:, 24704:24832].bitcast(F32)
        b3_t = big[:, 24832:24960].bitcast(F32)
        bs1_t = big[:, 24960:24992].bitcast(F32)
        bs3_t = big[:, 24992:25024].bitcast(F32)
        bs2_t = big[:, 25024:25040].bitcast(F32)
        b2_t = big[0:8, 25088:27136].bitcast(F32)
        sel8 = big[0:8, 27136:29184].bitcast(F32)
        gT = big[0:8, 29184:29184 + 2 * HALF].bitcast(F32)
        Gt = big[:, 31232:31232 + HALF]

        def cl(dst, src, waits=()):
            return S.op("sync", lambda h, d=dst, s=src: h.dma_start(out=d, in_=s),
                        waits=waits, inc="dX", incn=16)
        cl(sel16[:, :], sel16_h[:, :])
        cl(identb[:, :], identb_h[:, :])
        cl(identf[:, :], identf_h[:, :])
        cl(ones1[:, :], ones1_h[:, :])
        cl(gbrow[:, :], gb_h[:, :])
        cl(cw_t[:, :], cw_h[:, :])
        cl(cb_t[:, :], cb_h[:, :])
        cl(bdt_t[:, :], bdt_h[:, :])
        cl(a_t[:, :], a_h[:, :])
        cl(dv_t[:, :], dv_h[:, :])
        cl(fc[:, :], fc_h[:, :])

        tok_hc0 = S.op("vector", lambda h: h.memset(hc[:, :], 0.0), inc="sV")

        with ExitStack() as p0:
            pec = p0.enter_context
            x_tm = [pec(nc.sbuf_tensor(f"x_tm{i}", [128, D], F32)) for i in range(2)]
            xibf = [pec(nc.sbuf_tensor(f"xibf{i}", [128, D], BF16)) for i in range(2)]
            ncol = [pec(nc.sbuf_tensor(f"ncol{i}", [128, 1], F32)) for i in range(2)]
            nmean = [pec(nc.sbuf_tensor(f"nmean{i}", [128, 1], F32)) for i in range(2)]
            ptb = [pec(nc.psum_tensor(f"ptb{i}", [128, 128], BF16)) for i in range(2)]

            ptb_free = [None, None]
            act_toks = [None, None]
            for r in range(NT):
                b = r % 2
                ld = S.op("sync",
                          lambda h, b=b, r=r: h.dma_start(out=x_tm[b][:, :], in_=x_h[r * 128:(r + 1) * 128, :]),
                          waits=[act_toks[b]], inc="dX", incn=16)
                t1 = S.op("vector",
                          lambda h, b=b: h.reduce_sum(ncol[b][:, :], x_tm[b][:, :], axis=AX.X, negate=True),
                          waits=[ld], inc="sV")
                t2 = S.op("vector",
                          lambda h, b=b: h.tensor_single_scalar(nmean[b][:, :], ncol[b][:, :], 1.0 / D, ALU.mult),
                          waits=[t1], inc="sV")
                act_toks[b] = S.op("scalar",
                                   lambda h, b=b: h.activation(xibf[b][:, :], x_tm[b][:, :], AF.Identity,
                                                               bias=nmean[b][:, 0:1]),
                                   waits=[t2], inc="sA")
                for dc in range(8):
                    pb = dc % 2
                    tt = S.op("tensor",
                              lambda h, b=b, dc=dc, pb=pb: h.transpose(
                                  ptb[pb][:, :], xibf[b][:, dc * 128:(dc + 1) * 128], identb[:, :]),
                              waits=[act_toks[b], ptb_free[pb]], inc="sT")
                    ptb_free[pb] = S.op(
                        "vector",
                        lambda h, dc=dc, r=r, pb=pb: h.tensor_copy(
                            actfm[:, dc * T + r * 128: dc * T + (r + 1) * 128], ptb[pb][:, :]),
                        waits=[tt], inc="sV")
            p0_done = S.snap()

        sz_spill = {}
        u_done = {}
        with ExitStack() as p1:
            pec = p1.enter_context
            xz = [pec(nc.sbuf_tensor(f"xz{i}", [128, T + 4], BF16)) for i in range(2)]
            diag = [pec(nc.sbuf_tensor(f"diag{i}", [128, 4 * 128], BF16)) for i in range(2)]
            szt = [pec(nc.sbuf_tensor(f"szt{i}", [128, 512], BF16)) for i in range(2)]
            pz = [pec(nc.psum_tensor(f"pz{i}", [128, 512], F32)) for i in range(2)]
            pc = [pec(nc.psum_tensor(f"pc{i}", [128, 512], F32)) for i in range(2)]

            zpad = []
            for i in range(2):
                S.op("vector", lambda h, i=i: h.memset(xz[i][:, 0:1], 0.0), inc="sV")
                zpad.append(S.op("vector", lambda h, i=i: h.memset(xz[i][:, T + 1:T + 4], 0.0), inc="sV"))

            wz_tok = S.op("gpsimd",
                          lambda h: h.dma_start(out=wa[:, :].rearrange("p (k m) -> p k m", m=DI),
                                                in_=w_in_h[:, DI:2 * DI].rearrange("(k p) m -> p k m", p=128)),
                          inc="dW", incn=16)
            wxb_tok = S.op("gpsimd",
                           lambda h: h.dma_start(out=wb[:, :].rearrange("p (k m) -> p k m", m=DI),
                                                 in_=w_in_h[:, 0:DI].rearrange("(k p) m -> p k m", p=128)),
                           inc="dW", incn=16)

            pz_free = [None, None]
            pc_free = [None, None]
            szt_free = [None, None]

            for mz in range(16):
                for tc in range(NTC):
                    pb = (mz * NTC + tc) % 2
                    sb = pb
                    mt = S.op("tensor",
                              lambda h, mz=mz, tc=tc, pb=pb: [
                                  h.matmul(pz[pb][:, :], wa[:, k * DI + mz * 128:k * DI + (mz + 1) * 128],
                                           actfm[:, k * T + tc * 512:k * T + (tc + 1) * 512],
                                           start=(k == 0), stop=(k == 7)) for k in range(8)][-1],
                              waits=[wz_tok, pz_free[pb]] + (p0_done if mz == 0 and tc == 0 else []),
                              inc="sT")
                    sil = S.op("scalar",
                               lambda h, pb=pb, sb=sb: h.activation(szt[sb][:, :], pz[pb][:, :], AF.Silu),
                               waits=[mt, szt_free[sb]], inc="sA")
                    pz_free[pb] = sil
                    sz_spill[(mz, tc)] = S.op("scalar",
                                              lambda h, sb=sb, mz=mz, tc=tc: h.dma_start(
                                                  out=sz_d[mz * 128:(mz + 1) * 128, tc * 512:(tc + 1) * 512],
                                                  in_=szt[sb][:, :]),
                                              inc="dSp", incn=16)
                    szt_free[sb] = sz_spill[(mz, tc)]

            xz_free = [None, None]
            dg_free = [None, None]
            for mx in range(16):
                b = mx % 2
                ev = []
                for tc in range(NTC):
                    pb = tc % 2
                    mt = S.op("tensor",
                              lambda h, mx=mx, tc=tc, pb=pb: [
                                  h.matmul(pz[pb][:, :], wb[:, k * DI + mx * 128:k * DI + (mx + 1) * 128],
                                           actfm[:, k * T + tc * 512:k * T + (tc + 1) * 512],
                                           start=(k == 0), stop=(k == 7)) for k in range(8)][-1],
                              waits=[wxb_tok, pz_free[pb]] + ([xz_free[b]] if tc == 0 else []),
                              inc="sT")
                    e = S.op("vector",
                             lambda h, b=b, tc=tc, pb=pb: h.tensor_copy(
                                 xz[b][:, 1 + tc * 512:1 + (tc + 1) * 512], pz[pb][:, :]),
                             waits=[mt], inc="sV")
                    pz_free[pb] = e
                    ev.append(e)
                dg = None
                for k in range(4):
                    dg = S.op("vector",
                              lambda h, b=b, mx=mx, k=k: h.tensor_scalar(
                                  diag[b][:, k * 128:(k + 1) * 128], identb[:, :],
                                  cw_t[:, mx * 4 + k:mx * 4 + k + 1], None, ALU.mult),
                              waits=[dg_free[b]] if k == 0 else [], inc="sV")
                for tc in range(NTC):
                    pb = tc % 2
                    ct = S.op("tensor",
                              lambda h, b=b, tc=tc, pb=pb: [
                                  h.matmul(pc[pb][:, :], diag[b][:, k * 128:(k + 1) * 128],
                                           xz[b][:, tc * 512 + k: tc * 512 + k + 512],
                                           start=(k == 0), stop=(k == 3)) for k in range(4)][-1],
                              waits=[ev[-1], dg, zpad[b], pc_free[pb]], inc="sT")
                    if tc == NTC - 1:
                        xz_free[b] = ct
                        dg_free[b] = ct
                    u_done[(mx, tc)] = S.op("scalar",
                                            lambda h, mx=mx, tc=tc, pb=pb: h.activation(
                                                big[:, mx * T + tc * 512: mx * T + (tc + 1) * 512],
                                                pc[pb][:, :], AF.Silu, bias=cb_t[:, mx:mx + 1]),
                                            waits=[ct], inc="sA")
                    pc_free[pb] = u_done[(mx, tc)]
            m1_done = S.snap()
            if DEBUG:
                S.op("sync", lambda h: h.dma_start(
                    out=dbg["dbg_u"].rearrange("(k p) m -> p k m", p=128),
                    in_=big[:, 0:16 * T].rearrange("p (k m) -> p k m", m=T)),
                    waits=m1_done, inc="dX", incn=16)

        ddma = {}
        with ExitStack() as p2a:
            pec = p2a.enter_context
            wx_t = pec(nc.sbuf_tensor("wx_t", [128, 16 * 32], BF16))
            tmp1 = pec(nc.sbuf_tensor("tmp1", [128, 512], F32))
            tmp2 = pec(nc.sbuf_tensor("tmp2", [128, 512], F32))
            tmp3 = pec(nc.sbuf_tensor("tmp3", [128, 512], F32))
            dtt = [pec(nc.sbuf_tensor(f"dtt{i}", [128, 512], F32)) for i in range(2)]
            pd = [pec(nc.psum_tensor(f"pd{i}", [128, 512], F32)) for i in range(2)]
            pbcB = pec(nc.psum_tensor("pbcB", [16, 512], F32))
            pbcC = pec(nc.psum_tensor("pbcC", [16, 512], F32))

            wdtA = S.op("gpsimd",
                        lambda h: h.dma_start(out=wa[:, :].rearrange("p (k m) -> p k m", m=DI),
                                              in_=wdt_h[0:1024, :].rearrange("(k p) m -> p k m", p=128)),
                        waits=m1_done, inc="dW", incn=16)
            wdtB = S.op("gpsimd",
                        lambda h: h.dma_start(out=wb[:, :].rearrange("p (k m) -> p k m", m=DI),
                                              in_=wdt_h[1024:2048, :].rearrange("(k p) m -> p k m", p=128)),
                        inc="dW", incn=16)
            wxT = S.op("gpsimd",
                       lambda h: h.dma_start(out=wx_t[:, :].rearrange("p (k m) -> p k m", m=32),
                                             in_=wx_h[:, :].rearrange("(k p) m -> p k m", p=128)),
                       inc="dW", incn=16)

            bc_done = []
            pbcB_free = None
            pbcC_free = None
            for tc in range(NTC):
                mb = S.op("tensor",
                          lambda h, tc=tc: [h.matmul(pbcB[:, :], wx_t[:, k * 32:k * 32 + 16],
                                                     big[:, k * T + tc * 512:k * T + (tc + 1) * 512],
                                                     start=(k == 0), stop=(k == 15)) for k in range(16)][-1],
                          waits=[wxT, pbcB_free] + (m1_done if tc == 0 else []), inc="sT")
                eb = S.op("vector", lambda h, tc=tc: h.tensor_copy(braw[:, tc * 512:(tc + 1) * 512], pbcB[:, :]),
                          waits=[mb], inc="sV")
                pbcB_free = eb
                mc = S.op("tensor",
                          lambda h, tc=tc: [h.matmul(pbcC[:, :], wx_t[:, k * 32 + 16:k * 32 + 32],
                                                     big[:, k * T + tc * 512:k * T + (tc + 1) * 512],
                                                     start=(k == 0), stop=(k == 15)) for k in range(16)][-1],
                          waits=[pbcC_free], inc="sT")
                ecv = S.op("vector", lambda h, tc=tc: h.tensor_copy(craw[:, tc * 512:(tc + 1) * 512], pbcC[:, :]),
                           waits=[mc], inc="sV")
                pbcC_free = ecv
                bc_done += [eb, ecv]
            if DEBUG:
                S.op("sync", lambda h: h.dma_start(out=dbg["dbg_bc"][:, 0:T], in_=braw), waits=bc_done,
                     inc="dX", incn=16)
                S.op("sync", lambda h: h.dma_start(out=dbg["dbg_bc"][:, T:2 * T], in_=craw), waits=bc_done,
                     inc="dX", incn=16)

            pd_free = [None, None]
            dtt_free = [None, None]
            v1prev = [None]
            for md in range(16):
                for tc in range(NTC):
                    pb = (md * NTC + tc) % 2
                    mt = S.op("tensor",
                              lambda h, md=md, tc=tc, pb=pb: [
                                  h.matmul(pd[pb][:, :],
                                           (wa if k < 8 else wb)[:, (k % 8) * DI + md * 128:(k % 8) * DI + (md + 1) * 128],
                                           big[:, k * T + tc * 512:k * T + (tc + 1) * 512],
                                           start=(k == 0), stop=(k == 15)) for k in range(16)][-1],
                              waits=[wdtA, wdtB, pd_free[pb]] +
                                    ([bc_done[-1]] if md == 0 and tc == 0 else []),
                              inc="sT")
                    a1 = S.op("scalar",
                              lambda h, md=md, pb=pb: h.activation(tmp1[:, :], pd[pb][:, :], AF.Abs,
                                                                   bias=bdt_t[:, md:md + 1]),
                              waits=[mt, v1prev[0]], inc="sA")
                    a2 = S.op("scalar", lambda h: h.activation(tmp2[:, :], tmp1[:, :], AF.Exp, scale=-1.0),
                              inc="sA")
                    a3 = S.op("scalar", lambda h: h.activation(tmp1[:, :], tmp2[:, :], AF.Ln, bias=1.0),
                              inc="sA")
                    a4 = S.op("scalar",
                              lambda h, md=md, pb=pb: h.activation(tmp3[:, :], pd[pb][:, :], AF.Relu,
                                                                   bias=bdt_t[:, md:md + 1]),
                              inc="sA")
                    pd_free[pb] = a4
                    db = pb
                    v1 = S.op("vector",
                              lambda h, db=db: h.tensor_add(dtt[db][:, :], tmp1[:, :], tmp3[:, :]),
                              waits=[a4, dtt_free[db]], inc="sV")
                    v1prev[0] = v1
                    ddma[(md, tc)] = S.op("scalar",
                                          lambda h, md=md, tc=tc, db=db: h.dma_start(
                                              out=delta_d[md * 128:(md + 1) * 128, tc * 512:(tc + 1) * 512],
                                              in_=dtt[db][:, :]),
                                          waits=[v1], inc="dSp", incn=16)
                    dtt_free[db] = ddma[(md, tc)]
            m2a_done = S.snap()
            if DEBUG:
                S.op("sync", lambda h: h.dma_start(out=dbg["dbg_delta"][:, :], in_=delta_d[:, :]),
                     waits=m2a_done, inc="dX", incn=16)

        ybf_spill = {}
        with ExitStack() as p2b:
            pec = p2b.enter_context
            dtl = [pec(nc.sbuf_tensor(f"dtl{i}", [128, 512], F32)) for i in range(2)]
            dut = [pec(nc.sbuf_tensor(f"dut{i}", [128, 512], BF16)) for i in range(2)]
            dAt = [pec(nc.sbuf_tensor(f"dAt{i}", [128, 512], F32)) for i in range(2)]
            dBt = [pec(nc.sbuf_tensor(f"dBt{i}", [128, 512], BF16)) for i in range(2)]
            ht = [pec(nc.sbuf_tensor(f"ht{i}", [128, 512], BF16)) for i in range(2)]
            pt = [pec(nc.sbuf_tensor(f"pt{i}", [128, 512], BF16)) for i in range(2)]
            yac = [pec(nc.sbuf_tensor(f"yac{i}", [128, 512], F32)) for i in range(2)]
            szs = [pec(nc.sbuf_tensor(f"szs{i}", [128, 512], BF16)) for i in range(2)]
            ybt = [pec(nc.sbuf_tensor(f"ybt{i}", [128, 512], BF16)) for i in range(2)]
            pbb = [pec(nc.psum_tensor(f"pbb{i}", [128, 512], F32)) for i in range(2)]

            woutT = S.op("gpsimd",
                         lambda h: h.dma_start(out=wa[:, :].rearrange("p (k m) -> p k m", m=1024),
                                               in_=wout_h[:, :].rearrange("(k p) m -> p k m", p=128)),
                         waits=m2a_done, inc="dW", incn=16)

            dA_free = [None] * 2
            dB_free = [None] * 2
            h_free = [None] * 2
            p_free = [None] * 2
            ydma = [None, None]
            dt_free = [None, None]
            sz_free = [None, None]
            pbb_free = [None, None]
            for tc in range(NTC):
                bcast = []
                for s in range(16):
                    for (raw, dst, pbi) in ((braw, Bbc, 0), (craw, Cbc, 1)):
                        mt = S.op("tensor",
                                  lambda h, s=s, tc=tc, raw=raw, pbi=pbi: h.matmul(
                                      pbb[pbi][:, :], sel16[:, s * 128:(s + 1) * 128],
                                      raw[:, tc * 512:(tc + 1) * 512], start=True, stop=True),
                                  waits=[pbb_free[pbi]] + (m2a_done if tc == 0 and s == 0 else []),
                                  inc="sT")
                        e = S.op("vector",
                                 lambda h, s=s, dst=dst, pbi=pbi: h.tensor_copy(
                                     dst[:, s * 512:(s + 1) * 512], pbb[pbi][:, :]),
                                 waits=[mt], inc="sV")
                        pbb_free[pbi] = e
                        bcast.append(e)
                for mx in range(16):
                    db = mx % 2
                    dld = S.op("sync",
                               lambda h, mx=mx, tc=tc, db=db: h.dma_start(
                                   out=dtl[db][:, :],
                                   in_=delta_d[mx * 128:(mx + 1) * 128, tc * 512:(tc + 1) * 512]),
                               waits=[ddma[(mx, tc)], dt_free[db]], inc="dX", incn=16)
                    sld = S.op("sync",
                               lambda h, mx=mx, tc=tc, db=db: h.dma_start(
                                   out=szs[db][:, :],
                                   in_=sz_d[mx * 128:(mx + 1) * 128, tc * 512:(tc + 1) * 512]),
                               waits=[sz_spill[(mx, tc)], sz_free[db]], inc="dX", incn=16)
                    dtok = S.op("vector",
                                lambda h, mx=mx, tc=tc, db=db: h.tensor_tensor(
                                    dut[db][:, :], dtl[db][:, :],
                                    big[:, mx * T + tc * 512:mx * T + (tc + 1) * 512], ALU.mult),
                                waits=[dld], inc="sV")
                    ylast = None
                    for s in range(16):
                        j = s % 2
                        adA = S.op("scalar",
                                   lambda h, mx=mx, s=s, j=j, db=db: h.activation(
                                       dAt[j][:, :], dtl[db][:, :], AF.Exp,
                                       scale=a_t[:, mx * DS + s:mx * DS + s + 1]),
                                   waits=[dld, dA_free[j]], inc="sA")
                        S.op("vector",
                             lambda h, s=s, j=j, db=db: h.tensor_tensor(
                                 dBt[j][:, :], dut[db][:, :], Bbc[:, s * 512:(s + 1) * 512], ALU.mult),
                             waits=[dtok, dB_free[j], bcast[2 * s]], inc="sV")
                        hsc = S.op("vector",
                                   lambda h, mx=mx, s=s, j=j: h.tensor_tensor_scan(
                                       ht[j][:, :], dAt[j][:, :], dBt[j][:, :],
                                       hc[:, mx * DS + s:mx * DS + s + 1], ALU.mult, ALU.add),
                                   waits=[adA, h_free[j], tok_hc0], inc="sV")
                        dA_free[j] = hsc
                        dB_free[j] = hsc
                        S.op("vector",
                             lambda h, mx=mx, s=s, j=j: h.tensor_copy(
                                 hc[:, mx * DS + s:mx * DS + s + 1], ht[j][:, 511:512]),
                             waits=[hsc], inc="sV")
                        pm = S.op("vector",
                                  lambda h, s=s, j=j: h.tensor_tensor(
                                      pt[j][:, :], ht[j][:, :], Cbc[:, s * 512:(s + 1) * 512], ALU.mult),
                                  waits=[hsc, p_free[j], bcast[2 * s + 1]], inc="sV")
                        h_free[j] = pm
                        if s == 0:
                            ylast = S.op("gpsimd",
                                         lambda h, j=j: h.tensor_copy(yac[0][:, :], pt[j][:, :]),
                                         waits=[pm], inc="sP")
                        else:
                            src, dst = (yac[0], yac[1]) if s % 2 == 1 else (yac[1], yac[0])
                            ylast = S.op("gpsimd",
                                         lambda h, j=j, src=src, dst=dst: h.tensor_tensor(
                                             dst[:, :], src[:, :], pt[j][:, :], ALU.add),
                                         waits=[pm], inc="sP")
                        p_free[j] = ylast
                    y1 = S.op("vector",
                              lambda h, mx=mx, tc=tc: h.scalar_tensor_tensor(
                                  yac[1][:, :], big[:, mx * T + tc * 512:mx * T + (tc + 1) * 512],
                                  dv_t[:, mx:mx + 1], yac[1][:, :], ALU.mult, ALU.add),
                              waits=[ylast], inc="sV")
                    yo = mx % 2
                    y2 = S.op("gpsimd",
                              lambda h, yo=yo, db=db: h.tensor_tensor(
                                  ybt[yo][:, :], yac[1][:, :], szs[db][:, :], ALU.mult),
                              waits=[y1, sld, ydma[yo]], inc="sP")
                    sz_free[db] = y2
                    dt_free[db] = y2
                    ydma[yo] = S.op("scalar",
                                    lambda h, mx=mx, tc=tc, yo=yo: h.dma_start(
                                        out=ybf_d[mx * 128:(mx + 1) * 128, tc * 512:(tc + 1) * 512],
                                        in_=ybt[yo][:, :]),
                                    waits=[y2], inc="dSp", incn=16)
                    ybf_spill[(mx, tc)] = ydma[yo]
            m2b_done = S.snap()
            if DEBUG:
                S.op("sync", lambda h: h.dma_start(out=dbg["dbg_ybf"][:, :], in_=ybf_d[:, :]),
                     waits=m2b_done, inc="dX", incn=16)

        with ExitStack() as p3:
            pec = p3.enter_context
            otm = pec(nc.sbuf_tensor("otm", [128, D], F32))
            xrm = pec(nc.sbuf_tensor("xrm", [128, D], F32))
            x1tm = pec(nc.sbuf_tensor("x1tm", [128, D], F32))
            x1bf = pec(nc.sbuf_tensor("x1bf", [128, D], BF16))
            silu_t = [pec(nc.sbuf_tensor(f"silu{i}", [128, 512], BF16)) for i in range(2)]
            lg = pec(nc.sbuf_tensor("lg", [128, E], F32))
            sc_t = pec(nc.sbuf_tensor("sc_t", [128, E], F32))
            sc2_t = pec(nc.sbuf_tensor("sc2_t", [128, E], F32))
            mk = pec(nc.sbuf_tensor("mk", [128, E], F32))
            gtm = pec(nc.sbuf_tensor("gtm", [128, E], F32))
            c_ns = pec(nc.sbuf_tensor("c_ns", [128, 1], F32))
            c_mean = pec(nc.sbuf_tensor("c_mean", [128, 1], F32))
            c_sq = pec(nc.sbuf_tensor("c_sq", [128, 1], F32))
            c_var = pec(nc.sbuf_tensor("c_var", [128, 1], F32))
            c_rstd = pec(nc.sbuf_tensor("c_rstd", [128, 1], F32))
            ph1 = pec(nc.psum_tensor("ph1", [128, 512], F32))
            ph3 = pec(nc.psum_tensor("ph3", [128, 512], F32))
            py = [pec(nc.psum_tensor(f"py{i}", [128, 512], F32)) for i in range(2)]
            pf = [pec(nc.psum_tensor(f"pf{i}", [128, 512], F32)) for i in range(2)]
            pb_ = [pec(nc.psum_tensor(f"pb_{i}", [128, 128], BF16)) for i in range(2)]

            L = {}

            ffc = []
            ffc.append(cl(gwt_t, gwt_h[:, :], waits=m2b_done))
            ffc.append(cl(b1_t, b1t_h[:, :]))
            ffc.append(cl(b3_t, b3t_h[:, :]))
            ffc.append(cl(bs1_t, bs1_h[:, :]))
            ffc.append(cl(bs3_t, bs3_h[:, :]))
            ffc.append(cl(bs2_t, bs2_h[:, :]))
            ffc.append(cl(b2_t, b2_h[:, :]))
            ffc.append(cl(sel8, sel8_h[:, :]))
            ffc_done = ffc[-1]

            gb1 = S.op("tensor", lambda h: h.matmul(pf[0][:, 0:E], ones1[:, :], gbrow[:, :],
                                                    start=True, stop=True),
                       waits=m2b_done, inc="sT")
            gb2 = S.op("vector", lambda h: h.tensor_copy(gbbc[:, :], pf[0][:, 0:E]),
                       waits=[gb1], inc="sV")
            L[("pf", 0)] = gb2

            x1_spill = {}
            ln1_done = {}

            def do_m3_half(hh, wout_tok):
                for tcl in range(HTC):
                    tc = hh * HTC + tcl
                    ytile = []
                    for k in range(16):
                        ytile.append(S.op("sync",
                                          lambda h, k=k, tc=tc: h.dma_start(
                                              out=ybs[:, k * 512:(k + 1) * 512],
                                              in_=ybf_d[k * 128:(k + 1) * 128, tc * 512:(tc + 1) * 512]),
                                          waits=[ybf_spill[(k, tc)], L.get(("ybs", k)), L.get("he_mm")] +
                                                (m2b_done if hh == 0 and tcl == 0 and k == 0 else []),
                                          inc="dX", incn=16))
                    ofm_tok = {}
                    for mo in range(8):
                        pbk = mo % 2
                        mt = S.op("tensor",
                                  lambda h, mo=mo, pbk=pbk: [
                                      h.matmul(py[pbk][:, :], wa[:, k * 1024 + mo * 128:k * 1024 + (mo + 1) * 128],
                                               ybs[:, k * 512:(k + 1) * 512],
                                               start=(k == 0), stop=(k == 15)) for k in range(16)][-1],
                                  waits=[wout_tok, L.get(("py", pbk))] + ytile, inc="sT")
                        L["wa_mm"] = mt
                        ofm_tok[mo] = S.op("vector",
                                           lambda h, mo=mo, pbk=pbk: h.tensor_copy(
                                               ofm[:, mo * 512:(mo + 1) * 512], py[pbk][:, :]),
                                           waits=[mt, L.get(("ofm-rd", mo)), L.get("he_mm")], inc="sV")
                        L[("py", pbk)] = ofm_tok[mo]
                    for k in range(16):
                        L[("ybs", k)] = ofm_tok[7]
                    for rl in range(4):
                        r = tc * 4 + rl
                        xr = S.op("sync",
                                  lambda h, r=r: h.dma_start(out=xrm[:, :], in_=x_h[r * 128:(r + 1) * 128, :]),
                                  waits=[L.get("xrm")], inc="dX", incn=16)
                        ot = None
                        for dc in range(8):
                            fb = dc % 2
                            tt = S.op("tensor",
                                      lambda h, dc=dc, rl=rl, fb=fb: h.transpose(
                                          pf[fb][:, 0:128],
                                          ofm[:, dc * 512 + rl * 128: dc * 512 + (rl + 1) * 128],
                                          identf[:, :]),
                                      waits=[ofm_tok[dc], L.get(("pf", fb))], inc="sT")
                            L[("ofm-rd", dc)] = tt
                            L["ofm-last"] = tt
                            ot = S.op("vector",
                                      lambda h, dc=dc, fb=fb: h.scalar_tensor_tensor(
                                          otm[:, dc * 128:(dc + 1) * 128],
                                          xrm[:, dc * 128:(dc + 1) * 128],
                                          2.0, pf[fb][:, 0:128], ALU.mult, ALU.add),
                                      waits=[tt, xr, L.get("otm")], inc="sV")
                            L[("pf", fb)] = ot
                        L["xrm"] = ot
                        s1 = S.op("vector",
                                  lambda h: h.reduce_sum(c_ns[:, :], otm[:, :], axis=AX.X, negate=True),
                                  waits=[ot], inc="sV")
                        s2 = S.op("vector",
                                  lambda h: h.tensor_single_scalar(c_mean[:, :], c_ns[:, :], -1.0 / D, ALU.mult),
                                  waits=[s1], inc="sV")
                        s3 = S.op("scalar",
                                  lambda h: h.activation(x1tm[:, :], otm[:, :], AF.Square,
                                                         accum_out=c_sq[:, 0:1]),
                                  waits=[ot, L.get("x1tm"), L.get("x1sp"), L.get("outdma")], inc="sA")
                        s4 = S.op("scalar", lambda h: h.activation(c_var[:, :], c_mean[:, :], AF.Square),
                                  waits=[s2], inc="sA")
                        s5 = S.op("vector",
                                  lambda h: h.tensor_scalar(c_ns[:, :], c_sq[:, :], 1.0 / D,
                                                            c_var[:, 0:1], ALU.mult, ALU.subtract),
                                  waits=[s3, s4], inc="sV")
                        s6 = S.op("scalar", lambda h: h.activation(c_rstd[:, :], c_ns[:, :], AF.Ln, bias=fc[:, 0:1]),
                                  waits=[s5], inc="sA")
                        s7 = S.op("scalar", lambda h: h.activation(c_rstd[:, :], c_rstd[:, :], AF.Exp, scale=fc[:, 1:2]),
                                  inc="sA")
                        ap = S.op("vector",
                                  lambda h: h.tensor_scalar(x1tm[:, :], otm[:, :],
                                                            c_mean[:, 0:1], c_rstd[:, 0:1],
                                                            ALU.subtract, ALU.mult),
                                  waits=[s7, s3, s2], inc="sV")
                        L["otm"] = ap
                        x1_spill[r] = S.op("sync",
                                           lambda h, r=r: h.dma_start(out=x1_d[r * 128:(r + 1) * 128, :],
                                                                      in_=x1tm[:, :]),
                                           waits=[ap], inc="dX", incn=16)
                        L["x1sp"] = x1_spill[r]
                        cbt = S.op("vector", lambda h: h.tensor_copy(x1bf[:, :], x1tm[:, :]),
                                   waits=[ap, L.get("x1bf")], inc="sV")
                        L["x1tm"] = cbt
                        if DEBUG:
                            S.op("sync", lambda h, r=r: h.dma_start(
                                out=dbg["dbg_x1"][r * 128:(r + 1) * 128, :], in_=x1tm[:, :]),
                                waits=[cbt], inc="dX", incn=16)
                        lastev = None
                        for dc in range(8):
                            bb = dc % 2
                            tt = S.op("tensor",
                                      lambda h, dc=dc, bb=bb: h.transpose(
                                          pb_[bb][:, :], x1bf[:, dc * 128:(dc + 1) * 128], identb[:, :]),
                                      waits=[cbt, L.get(("pb", bb))], inc="sT")
                            lastev = S.op("vector",
                                          lambda h, dc=dc, r=r, bb=bb: h.tensor_copy(
                                              actfm[:, dc * T + r * 128:dc * T + (r + 1) * 128], pb_[bb][:, :]),
                                          waits=[tt], inc="sV")
                            L[("pb", bb)] = lastev
                        L["x1bf"] = lastev
                        ln1_done[r] = lastev

            def do_ffn_half(hh):
                base_t = hh * HALF
                gate_toks = []
                for rl in range(NTH):
                    r = hh * NTH + rl
                    mt = S.op("tensor",
                              lambda h, r=r: [
                                  h.matmul(pf[0][:, 0:E], actfm[:, k * T + r * 128:k * T + (r + 1) * 128],
                                           gwt_t[:, k * E:(k + 1) * E],
                                           start=(k == 0), stop=(k == 7)) for k in range(8)][-1],
                              waits=[ln1_done[r], gb2, ffc_done, L.get(("pf", 0))], inc="sT")
                    g1 = S.op("vector", lambda h: h.tensor_add(lg[:, :], pf[0][:, 0:E], gbbc[:, :]),
                              waits=[mt, L.get("lg")], inc="sV")
                    L[("pf", 0)] = g1
                    g2 = S.op("vector", lambda h: h.reduce_max(c_ns[:, :], lg[:, :], axis=AX.X),
                              waits=[g1], inc="sV")
                    g2b = S.op("vector", lambda h: h.tensor_single_scalar(c_mean[:, :], c_ns[:, :], -1.0, ALU.mult),
                               waits=[g2], inc="sV")
                    g3 = S.op("scalar", lambda h: h.activation(sc_t[:, :], lg[:, :], AF.Exp, bias=c_mean[:, 0:1]),
                              waits=[g2b, L.get("sc_t")], inc="sA")
                    g4 = S.op("vector", lambda h: h.reduce_sum(c_sq[:, :], sc_t[:, :], axis=AX.X),
                              waits=[g3, g2b], inc="sV")
                    g5 = S.op("vector", lambda h: h.reciprocal(c_var[:, :], c_sq[:, :]), waits=[g4], inc="sV")
                    g6 = S.op("vector", lambda h: h.tensor_single_scalar(sc_t[:, :], sc_t[:, :],
                                                                         c_var[:, 0:1], ALU.mult), waits=[g5], inc="sV")
                    g7 = S.op("vector", lambda h: h.reduce_max(c_ns[:, :], sc_t[:, :], axis=AX.X),
                              waits=[g6], inc="sV")
                    g8 = S.op("vector", lambda h: h.tensor_single_scalar(mk[:, :], sc_t[:, :], c_ns[:, 0:1],
                                                                         ALU.is_ge), waits=[g7], inc="sV")
                    g9 = S.op("vector", lambda h: h.scalar_tensor_tensor(sc2_t[:, :], mk[:, :], -1e30, sc_t[:, :],
                                                                         ALU.mult, ALU.add), waits=[g8], inc="sV")
                    g10 = S.op("vector", lambda h: h.reduce_max(c_ns[:, :], sc2_t[:, :], axis=AX.X),
                               waits=[g9], inc="sV")
                    g11 = S.op("vector", lambda h: h.tensor_single_scalar(mk[:, :], sc_t[:, :], c_ns[:, 0:1],
                                                                          ALU.is_ge), waits=[g10], inc="sV")
                    g12 = S.op("vector", lambda h: h.tensor_mul(gtm[:, :], sc_t[:, :], mk[:, :]),
                               waits=[g11, L.get("gdump")], inc="sV")
                    L["lg"] = g12
                    L["sc_t"] = g12
                    if DEBUG:
                        L["gdump"] = S.op("sync", lambda h, r=r: h.dma_start(
                            out=dbg["dbg_g"][r * 128:(r + 1) * 128, :], in_=gtm[:, :]),
                            waits=[g12], inc="dX", incn=16)
                    tg = S.op("tensor", lambda h: h.transpose(pf[1][0:8, 0:128], gtm[:, :], identf[:, :]),
                              waits=[g12, L.get(("pf", 1))], inc="sT")
                    gev = S.op("vector",
                               lambda h, rl=rl: h.tensor_copy(gT[0:8, rl * 128:(rl + 1) * 128],
                                                              pf[1][0:8, 0:128]),
                               waits=[tg, L.get("gtm"), ffc_done], inc="sV")
                    L[("pf", 1)] = gev
                    L["gtm"] = gev
                    gate_toks.append(gev)
                gate_done = gate_toks[-1]

                py0_free = L.get(("py", 0))
                for tcl in range(HTC):
                    for mo in range(8):
                        mt = S.op("tensor",
                                  lambda h, tcl=tcl, mo=mo: h.matmul(
                                      py[0][:, :], b2_t[:, mo * 128:(mo + 1) * 128],
                                      gT[:, tcl * 512:(tcl + 1) * 512], start=True, stop=True),
                                  waits=[gate_done, py0_free], inc="sT")
                        py0_free = S.op("vector",
                                        lambda h, tcl=tcl, mo=mo: h.tensor_scalar(
                                            ysb[:, mo * HALF + tcl * 512: mo * HALF + (tcl + 1) * 512],
                                            py[0][:, :], bs2_t[:, mo:mo + 1], None, ALU.add),
                                        waits=[mt, L.get(("ysb", tcl, mo)), L.get("wa_mm")], inc="sV")
                        L[("ysb", tcl, mo)] = py0_free
                L[("py", 0)] = py0_free

                experts = []
                for e in range(E):
                    experts.append(dict(
                        w1=w1t_h[e * D:(e + 1) * D, :], w3=w3t_h[e * D:(e + 1) * D, :],
                        w2=w2t_h[e * IN:(e + 1) * IN, :],
                        b1c=[b1_t[:, e * 8 + mi:e * 8 + mi + 1] for mi in range(8)],
                        b3c=[b3_t[:, e * 8 + mi:e * 8 + mi + 1] for mi in range(8)],
                        gated=e))
                experts.append(dict(
                    w1=ws1t_h[:, 0:1024], w3=ws3t_h[:, 0:1024], w2=ws2t_h[0:1024, :],
                    b1c=[bs1_t[:, mi:mi + 1] for mi in range(8)],
                    b3c=[bs3_t[:, mi:mi + 1] for mi in range(8)], gated=None))
                experts.append(dict(
                    w1=ws1t_h[:, 1024:2048], w3=ws3t_h[:, 1024:2048], w2=ws2t_h[1024:2048, :],
                    b1c=[bs1_t[:, 8 + mi:8 + mi + 1] for mi in range(8)],
                    b3c=[bs3_t[:, 8 + mi:8 + mi + 1] for mi in range(8)], gated=None))

                pending = {}

                def load_one(j, dram_ap, wait_toks):
                    sl = j
                    tok = S.op("gpsimd",
                               lambda h, sl=sl, dram_ap=dram_ap: h.dma_start(
                                   out=big[:, sl * 8192: sl * 8192 + 8192].rearrange("p (k m) -> p k m", m=1024),
                                   in_=dram_ap.rearrange("(k p) m -> p k m", p=128)),
                               waits=wait_toks, inc="dW", incn=16)
                    return sl, tok

                first_w = list(m2b_done) if hh == 0 else [L.get("he_mm"), L.get("ph1"), L.get("ph3")]
                pending[0] = tuple(zip(*[load_one(j, experts[0][k], first_w)
                                         for j, k in enumerate(("w1", "w3", "w2"))]))

                for ie, ex in enumerate(experts):
                    sls, tks = pending.pop(ie)
                    (s1l, s3l, s2l), (t1, t3, t2) = sls, tks
                    gated = ex["gated"]
                    gtok = None
                    if gated is not None:
                        for tcl in range(HTC):
                            gm = S.op("tensor",
                                      lambda h, tcl=tcl, gated=gated: h.matmul(
                                          pf[0][:, :], sel8[:, gated * 128:(gated + 1) * 128],
                                          gT[:, tcl * 512:(tcl + 1) * 512], start=True, stop=True),
                                      waits=[gate_done, ffc_done, L.get(("pf", 0))], inc="sT")
                            gtok = S.op("vector",
                                        lambda h, tcl=tcl: h.tensor_copy(
                                            Gt[:, tcl * 512:(tcl + 1) * 512], pf[0][:, :]),
                                        waits=[gm, L.get("Gt")], inc="sV")
                            L[("pf", 0)] = gtok
                    he_free = L.get("he_mm")
                    ofm_guard = L.get("ofm-last")
                    hlast = {}
                    for tcl in range(HTC):
                        tgl = base_t + tcl * 512
                        for mi in range(8):
                            m1_ = S.op("tensor",
                                       lambda h, s1l=s1l, mi=mi, tgl=tgl: [
                                           h.matmul(ph1[:, :],
                                                    big[:, s1l * 8192 + k * 1024 + mi * 128:
                                                         s1l * 8192 + k * 1024 + (mi + 1) * 128],
                                                    actfm[:, k * T + tgl:k * T + tgl + 512],
                                                    start=(k == 0), stop=(k == 7)) for k in range(8)][-1],
                                       waits=[t1, L.get("ph1")], inc="sT")
                            m3_ = S.op("tensor",
                                       lambda h, s3l=s3l, mi=mi, tgl=tgl: [
                                           h.matmul(ph3[:, :],
                                                    big[:, s3l * 8192 + k * 1024 + mi * 128:
                                                         s3l * 8192 + k * 1024 + (mi + 1) * 128],
                                                    actfm[:, k * T + tgl:k * T + tgl + 512],
                                                    start=(k == 0), stop=(k == 7)) for k in range(8)][-1],
                                       waits=[t3, L.get("ph3")], inc="sT")
                            sb = mi % 2
                            sa = S.op("scalar",
                                      lambda h, mi=mi, sb=sb, ex=ex: h.activation(
                                          silu_t[sb][:, :], ph1[:, :], AF.Silu, bias=ex["b1c"][mi]),
                                      waits=[m1_, L.get(("silu", sb))], inc="sA")
                            L["ph1"] = sa
                            hv = S.op("vector",
                                      lambda h, mi=mi, ex=ex, sb=sb: h.scalar_tensor_tensor(
                                          he[:, mi * 512:(mi + 1) * 512], ph3[:, :], ex["b3c"][mi],
                                          silu_t[sb][:, :], ALU.add, ALU.mult),
                                      waits=[m3_, sa, he_free, ofm_guard], inc="sV")
                            L["ph3"] = hv
                            L[("silu", sb)] = hv
                            if gated is not None:
                                hv = S.op("vector",
                                          lambda h, mi=mi, tcl=tcl: h.tensor_tensor(
                                              he[:, mi * 512:(mi + 1) * 512], he[:, mi * 512:(mi + 1) * 512],
                                              Gt[:, tcl * 512:(tcl + 1) * 512], ALU.mult),
                                          waits=[gtok], inc="sV")
                            hlast[mi] = hv
                        wmts = []
                        for mo in range(8):
                            pbk = mo % 2
                            mt = S.op("tensor",
                                      lambda h, s2l=s2l, mo=mo, pbk=pbk: [
                                          h.matmul(py[pbk][:, :],
                                                   big[:, s2l * 8192 + k * 1024 + mo * 128:
                                                        s2l * 8192 + k * 1024 + (mo + 1) * 128],
                                                   he[:, k * 512:(k + 1) * 512],
                                                   start=(k == 0), stop=(k == 7)) for k in range(8)][-1],
                                      waits=[t2, L.get(("py", pbk))] + [hlast[k] for k in range(8)],
                                      inc="sT")
                            wmts.append(mt)
                            yt = S.op("vector",
                                      lambda h, mo=mo, tcl=tcl, pbk=pbk: h.tensor_tensor(
                                          ysb[:, mo * HALF + tcl * 512: mo * HALF + (tcl + 1) * 512],
                                          ysb[:, mo * HALF + tcl * 512: mo * HALF + (tcl + 1) * 512],
                                          py[pbk][:, :], ALU.add),
                                      waits=[mt, L.get(("ysb", tcl, mo))], inc="sV")
                            L[("py", pbk)] = yt
                            L[("ysb", tcl, mo)] = yt
                        he_free = wmts[-1]
                        L["he_mm"] = wmts[-1]
                    if gated is not None:
                        L["Gt"] = L["he_mm"]
                    if ie + 1 < len(experts):
                        nx = experts[ie + 1]
                        _, n1 = load_one(0, nx["w1"], [L["ph1"]])
                        _, n3 = load_one(1, nx["w3"], [L["ph3"]])
                        _, n2 = load_one(2, nx["w2"], [L["he_mm"]])
                        pending[ie + 1] = ((0, 1, 2), (n1, n3, n2))

                ln2_rd = []
                for rl in range(NTH):
                    r = hh * NTH + rl
                    tclr = rl // 4
                    xr = S.op("sync",
                              lambda h, r=r: h.dma_start(out=xrm[:, :], in_=x1_d[r * 128:(r + 1) * 128, :]),
                              waits=[x1_spill[r], L.get("xrm")], inc="dX", incn=16)
                    ot = None
                    for dc in range(8):
                        fb = dc % 2
                        tt = S.op("tensor",
                                  lambda h, dc=dc, rl=rl, fb=fb: h.transpose(
                                      pf[fb][:, 0:128],
                                      ysb[:, dc * HALF + rl * 128: dc * HALF + (rl + 1) * 128],
                                      identf[:, :]),
                                  waits=[L.get(("ysb", tclr, dc)), L.get(("pf", fb))], inc="sT")
                        L[("ysb", tclr, dc)] = tt
                        ln2_rd.append(tt)
                        ot = S.op("vector",
                                  lambda h, dc=dc, fb=fb: h.scalar_tensor_tensor(
                                      otm[:, dc * 128:(dc + 1) * 128], xrm[:, dc * 128:(dc + 1) * 128],
                                      1.0, pf[fb][:, 0:128], ALU.mult, ALU.add),
                                  waits=[tt, xr, L.get("otm")], inc="sV")
                        L[("pf", fb)] = ot
                    L["xrm"] = ot
                    s1 = S.op("vector", lambda h: h.reduce_sum(c_ns[:, :], otm[:, :], axis=AX.X, negate=True),
                              waits=[ot], inc="sV")
                    s2 = S.op("vector", lambda h: h.tensor_single_scalar(c_mean[:, :], c_ns[:, :], -1.0 / D, ALU.mult),
                              waits=[s1], inc="sV")
                    s3 = S.op("scalar", lambda h: h.activation(x1tm[:, :], otm[:, :], AF.Square,
                                                               accum_out=c_sq[:, 0:1]),
                              waits=[ot, L.get("x1tm"), L.get("outdma")], inc="sA")
                    s4 = S.op("scalar", lambda h: h.activation(c_var[:, :], c_mean[:, :], AF.Square),
                              waits=[s2], inc="sA")
                    s5 = S.op("vector", lambda h: h.tensor_scalar(c_ns[:, :], c_sq[:, :], 1.0 / D,
                                                                  c_var[:, 0:1], ALU.mult, ALU.subtract),
                              waits=[s3, s4], inc="sV")
                    s6 = S.op("scalar", lambda h: h.activation(c_rstd[:, :], c_ns[:, :], AF.Ln, bias=fc[:, 0:1]),
                              waits=[s5], inc="sA")
                    s7 = S.op("scalar", lambda h: h.activation(c_rstd[:, :], c_rstd[:, :], AF.Exp, scale=fc[:, 1:2]),
                              inc="sA")
                    ap = S.op("vector", lambda h: h.tensor_scalar(x1tm[:, :], otm[:, :], c_mean[:, 0:1],
                                                                  c_rstd[:, 0:1], ALU.subtract, ALU.mult),
                              waits=[s7, s3, s2], inc="sV")
                    L["otm"] = ap
                    L["x1tm"] = ap
                    L["outdma"] = S.op("sync",
                                       lambda h, r=r: h.dma_start(out=out_h[r * 128:(r + 1) * 128, :],
                                                                  in_=x1tm[:, :]),
                                       waits=[ap], inc="dO", incn=16)
                return ln2_rd

            do_m3_half(0, woutT)
            rd0 = do_ffn_half(0)
            woutT2 = S.op("gpsimd",
                          lambda h: h.dma_start(out=wa[:, :].rearrange("p (k m) -> p k m", m=1024),
                                                in_=wout_h[:, :].rearrange("(k p) m -> p k m", p=128)),
                          waits=rd0[-16:], inc="dW", incn=16)
            do_m3_half(1, woutT2)
            do_ffn_half(1)

            S.op("sync", lambda h: h.nop(), waits=[("dO", S.cnt["dO"])])

        with nc.Block() as block:
            S.emit(block)

    return nc


def _host_layout(ins):
    f = np.float32
    bb = ml_dtypes.bfloat16

    def tile_col(v, nk):
        v = np.asarray(v, f).reshape(nk, 128, -1).transpose(1, 0, 2)
        return np.ascontiguousarray(v.reshape(128, -1))

    w = {}
    w["w_in"] = np.ascontiguousarray(np.asarray(ins["L0_attn_W_in"], f))
    w["cw"] = tile_col(np.asarray(ins["L0_attn_conv_w"], f).reshape(4, DI).T, 16)
    w["cb"] = tile_col(np.asarray(ins["L0_attn_conv_b"], f).reshape(DI, 1), 16)
    w["w_x"] = np.ascontiguousarray(np.asarray(ins["L0_attn_W_x"], f))
    w["w_dt"] = np.ascontiguousarray(np.asarray(ins["L0_attn_W_dt"], f))
    w["b_dt"] = tile_col(np.asarray(ins["L0_attn_b_dt"], f).reshape(DI, 1), 16)
    w["w_out"] = np.ascontiguousarray(np.asarray(ins["L0_attn_W_out"], f))
    w["a_mat"] = tile_col(np.asarray(ins["L0_attn_A"], f), 16)
    w["d_vec"] = tile_col(np.asarray(ins["L0_attn_D"], f).reshape(DI, 1), 16)
    w["gate_wt"] = tile_col(np.ascontiguousarray(np.asarray(ins["L0_ffn_gate_w"], f).T), 8).astype(bb)
    w["gate_b"] = np.asarray(ins["L0_ffn_gate_b"], f).reshape(1, E)
    w["w1t"] = np.ascontiguousarray(np.asarray(ins["L0_ffn_w1"], f).transpose(0, 2, 1).reshape(E * D, IN))
    w["w3t"] = np.ascontiguousarray(np.asarray(ins["L0_ffn_w3"], f).transpose(0, 2, 1).reshape(E * D, IN))
    w["w2t"] = np.ascontiguousarray(np.asarray(ins["L0_ffn_w2"], f).transpose(0, 2, 1).reshape(E * IN, D))
    b1 = np.asarray(ins["L0_ffn_b1"], f)
    b3 = np.asarray(ins["L0_ffn_b3"], f)
    bt1 = np.zeros((128, 8 * E), f)
    bt3 = np.zeros((128, 8 * E), f)
    for e in range(E):
        for mi in range(8):
            bt1[:, e * 8 + mi] = b1[e, mi * 128:(mi + 1) * 128]
            bt3[:, e * 8 + mi] = b3[e, mi * 128:(mi + 1) * 128]
    w["b1t"] = bt1
    w["b3t"] = bt3
    w["b2"] = np.ascontiguousarray(np.asarray(ins["L0_ffn_b2"], f))
    w["ws1t"] = np.ascontiguousarray(np.asarray(ins["L0_ffn_ws1"], f).T)
    w["bs1"] = tile_col(np.asarray(ins["L0_ffn_bs1"], f).reshape(SH, 1), 16)
    w["ws3t"] = np.ascontiguousarray(np.asarray(ins["L0_ffn_ws3"], f).T)
    w["bs3"] = tile_col(np.asarray(ins["L0_ffn_bs3"], f).reshape(SH, 1), 16)
    w["ws2t"] = np.ascontiguousarray(np.asarray(ins["L0_ffn_ws2"], f).T)
    w["bs2"] = tile_col(np.asarray(ins["L0_ffn_bs2"], f).reshape(D, 1), 8)
    sel16 = np.zeros((16, 16 * 128), bb)
    for s in range(16):
        sel16[s, s * 128:(s + 1) * 128] = 1
    w["sel16"] = sel16
    sel8 = np.zeros((8, 8 * 128), f)
    for s in range(8):
        sel8[s, s * 128:(s + 1) * 128] = 1
    w["sel8"] = sel8
    w["identb"] = np.eye(128, dtype=bb)
    w["identf"] = np.eye(128, dtype=f)
    w["ones1"] = np.ones((1, 128), f)
    fcv = np.zeros((128, 2), f)
    fcv[:, 0] = 1e-5
    fcv[:, 1] = -0.5
    w["fc"] = fcv
    return w


def kernel(**ins):
    T = ins["L0_attn_inp"].shape[1]
    w = _host_layout(ins)
    x = np.asarray(ins["L0_attn_inp"], np.float32)
    in_maps = []
    for b in range(NCORES):
        m = dict(w)
        m["x"] = np.ascontiguousarray(x[b])
        in_maps.append(m)
    nc = build_program(T)
    res = run_bass_kernel_spmd(nc, in_maps, core_ids=list(range(NCORES)))
    return np.stack([np.asarray(r["out"]) for r in res.results], axis=0)


# version 04
# speedup vs baseline: 1.0122x; 1.0122x over previous
import numpy as np
import ml_dtypes
import concourse.bass as bass
import concourse.mybir as mybir
from concourse.bass_utils import run_bass_kernel_spmd
from contextlib import ExitStack

F32 = mybir.dt.float32
BF16 = mybir.dt.bfloat16
AF = mybir.ActivationFunctionType
ALU = mybir.AluOpType
AX = mybir.AxisListType

D = 1024
DI = 2048
DS = 16
E = 8
IN = 1024
SH = 2048
NCORES = 8

DEBUG = False


class Sched:
    ENGS = ("sync", "scalar", "vector", "tensor", "gpsimd")

    def __init__(self, nc):
        self.nc = nc
        self.items = {e: [] for e in self.ENGS}
        self.sems = {}
        self.cnt = {}

    def sem(self, name):
        self.sems[name] = self.nc.alloc_semaphore(name)
        self.cnt[name] = 0

    QUEUE_LANES = {"sync": 8, "scalar": 6, "gpsimd": 6}

    def init_lanes(self):
        self.lanes = {}
        self.lane_rr = {}
        self.groups = {}
        for q, n in self.QUEUE_LANES.items():
            self.lanes[q] = []
            for i in range(n):
                name = f"dma_{q}_{i}"
                self.sem(name)
                self.lanes[q].append(name)
            self.lane_rr[q] = 0

    def op(self, eng, fn, waits=(), inc=None, incn=1):
        tok = None
        ws = [w for w in waits if w is not None]
        if inc is not None and incn == 16:
            group = inc
            lanes = self.lanes[eng]
            li = self.lane_rr[eng] % len(lanes)
            self.lane_rr[eng] += 1
            lsem = lanes[li]
            gate = self.cnt[lsem]
            if gate > 0:
                ws.append((lsem, gate))
            self.cnt[lsem] += 16
            tok = (lsem, self.cnt[lsem])
            self.groups.setdefault(group, {})[lsem] = self.cnt[lsem]
            self.items[eng].append((fn, tuple(ws), (lsem, 16)))
            return tok
        if inc is not None:
            self.cnt[inc] += incn
            tok = (inc, self.cnt[inc])
        self.items[eng].append((fn, tuple(ws), (inc, incn) if inc else None))
        return tok

    def group_waits(self, group):
        return [(s, v) for s, v in self.groups.get(group, {}).items()]

    def snap(self):
        return [(s, v) for s, v in self.cnt.items() if v > 0]

    def emit(self, block):
        for e in self.ENGS:
            items = self.items[e]
            if not items:
                continue

            def body(h, items=items):
                lastw = {}
                for fn, ws, inc in items:
                    for (s, v) in ws:
                        if lastw.get(s, 0) < v:
                            h.wait_ge(self.sems[s], v)
                            lastw[s] = v
                    ins = fn(h)
                    if inc is not None:
                        s, n = inc
                        ins.then_inc(self.sems[s], n)
            getattr(block, e)(body)


def build_program(T):
    nc = bass.Bass("TRN2", target_bir_lowering=False)
    NT = T // 128
    NTC = T // 512
    HALF = T // 2
    HTC = HALF // 512
    NTH = NT // 2

    def din(name, shape, dt=F32):
        return nc.dram_tensor(name, list(shape), dt, kind="ExternalInput")

    x_h = din("x", (T, D))
    w_in_h = din("w_in", (D, 2 * DI))
    cw_h = din("cw", (128, 16 * 4))
    cb_h = din("cb", (128, 16))
    wx_h = din("w_x", (DI, 2 * DS))
    wdt_h = din("w_dt", (DI, DI))
    bdt_h = din("b_dt", (128, 16))
    wout_h = din("w_out", (DI, D))
    a_h = din("a_mat", (128, 16 * DS))
    dv_h = din("d_vec", (128, 16))
    gwt_h = din("gate_wt", (128, 8 * E), BF16)
    gb_h = din("gate_b", (1, E))
    w1t_h = din("w1t", (E * D, IN))
    b1t_h = din("b1t", (128, 8 * E))
    w3t_h = din("w3t", (E * D, IN))
    b3t_h = din("b3t", (128, 8 * E))
    w2t_h = din("w2t", (E * IN, D))
    b2_h = din("b2", (E, D))
    ws1t_h = din("ws1t", (D, SH))
    bs1_h = din("bs1", (128, 16))
    ws3t_h = din("ws3t", (D, SH))
    bs3_h = din("bs3", (128, 16))
    ws2t_h = din("ws2t", (SH, D))
    bs2_h = din("bs2", (128, 8))
    sel16_h = din("sel16", (16, 16 * 128), BF16)
    sel8_h = din("sel8", (8, 8 * 128))
    identb_h = din("identb", (128, 128), BF16)
    identf_h = din("identf", (128, 128))
    ones1_h = din("ones1", (1, 128))
    fc_h = din("fc", (128, 2))

    out_h = nc.dram_tensor("out", [T, D], F32, kind="ExternalOutput")

    sz_d = nc.dram_tensor("sz_d", [DI, T], BF16, kind="Internal")
    delta_d = nc.dram_tensor("delta_d", [DI, T], F32, kind="Internal")
    ybf_d = nc.dram_tensor("ybf_d", [DI, T], BF16, kind="Internal")
    x1_d = nc.dram_tensor("x1_d", [T, D], F32, kind="Internal")

    dbg = {}
    if DEBUG:
        for name, shape, dt in [
            ("dbg_u", (DI, T), BF16), ("dbg_delta", (DI, T), F32),
            ("dbg_bc", (16, 2 * T), BF16), ("dbg_x1", (T, D), F32),
            ("dbg_ybf", (DI, T), BF16),
            ("dbg_g", (T, E), F32),
        ]:
            dbg[name] = nc.dram_tensor(name, list(shape), dt, kind="ExternalOutput")

    S = Sched(nc)
    for s in ("sT", "sV", "sA", "sP"):
        S.sem(s)
    S.init_lanes()

    with ExitStack() as st:
        ec = st.enter_context

        sel16 = ec(nc.sbuf_tensor("sel16_s", [16, 16 * 128], BF16))
        identb = ec(nc.sbuf_tensor("identb_s", [128, 128], BF16))
        identf = ec(nc.sbuf_tensor("identf_s", [128, 128], F32))
        ones1 = ec(nc.sbuf_tensor("ones1_s", [1, 128], F32))
        gbrow = ec(nc.sbuf_tensor("gbrow", [1, E], F32))
        gbbc = ec(nc.sbuf_tensor("gbbc", [128, E], F32))
        cw_t = ec(nc.sbuf_tensor("cw_t", [128, 16 * 4], F32))
        cb_t = ec(nc.sbuf_tensor("cb_t", [128, 16], F32))
        bdt_t = ec(nc.sbuf_tensor("bdt_t", [128, 16], F32))
        a_t = ec(nc.sbuf_tensor("a_t", [128, 16 * DS], F32))
        dv_t = ec(nc.sbuf_tensor("dv_t", [128, 16], F32))
        hc = ec(nc.sbuf_tensor("hc", [128, 16 * DS], F32))
        fc = ec(nc.sbuf_tensor("fc_s", [128, 2], F32))
        actfm = ec(nc.sbuf_tensor("actfm", [128, 8 * T], BF16))
        big = ec(nc.sbuf_tensor("big", [128, 32768], BF16))
        wa = ec(nc.sbuf_tensor("wa", [128, 16384], BF16))
        wb = ec(nc.sbuf_tensor("wb", [128, 16384], BF16))

        braw = actfm[0:16, 0:T]
        craw = actfm[0:16, T:2 * T]
        Bbc = wb[:, 0:8192]
        Cbc = wb[:, 8192:16384]
        ysb = wa[:, 0:16384].bitcast(F32)
        ybs = wb[:, 0:8192]
        ofm = wb[:, 8192:16384].bitcast(F32)
        he = wb[:, 8192:16384]
        gwt_t = big[:, 24576:24640]
        b1_t = big[:, 24704:24832].bitcast(F32)
        b3_t = big[:, 24832:24960].bitcast(F32)
        bs1_t = big[:, 24960:24992].bitcast(F32)
        bs3_t = big[:, 24992:25024].bitcast(F32)
        bs2_t = big[:, 25024:25040].bitcast(F32)
        b2_t = big[0:8, 25088:27136].bitcast(F32)
        sel8 = big[0:8, 27136:29184].bitcast(F32)
        gT = big[0:8, 29184:29184 + 2 * HALF].bitcast(F32)
        Gt = big[:, 31232:31232 + HALF]

        def cl(dst, src, waits=()):
            return S.op("sync", lambda h, d=dst, s=src: h.dma_start(out=d, in_=s),
                        waits=waits, inc="dX", incn=16)
        cl(sel16[:, :], sel16_h[:, :])
        cl(identb[:, :], identb_h[:, :])
        cl(identf[:, :], identf_h[:, :])
        cl(ones1[:, :], ones1_h[:, :])
        cl(gbrow[:, :], gb_h[:, :])
        cl(cw_t[:, :], cw_h[:, :])
        cl(cb_t[:, :], cb_h[:, :])
        cl(bdt_t[:, :], bdt_h[:, :])
        cl(a_t[:, :], a_h[:, :])
        cl(dv_t[:, :], dv_h[:, :])
        cl(fc[:, :], fc_h[:, :])

        tok_hc0 = S.op("vector", lambda h: h.memset(hc[:, :], 0.0), inc="sV")

        with ExitStack() as p0:
            pec = p0.enter_context
            x_tm = [pec(nc.sbuf_tensor(f"x_tm{i}", [128, D], F32)) for i in range(2)]
            xibf = [pec(nc.sbuf_tensor(f"xibf{i}", [128, D], BF16)) for i in range(2)]
            ncol = [pec(nc.sbuf_tensor(f"ncol{i}", [128, 1], F32)) for i in range(2)]
            nmean = [pec(nc.sbuf_tensor(f"nmean{i}", [128, 1], F32)) for i in range(2)]
            ptb = [pec(nc.psum_tensor(f"ptb{i}", [128, 128], BF16)) for i in range(2)]

            ptb_free = [None, None]
            act_toks = [None, None]
            for r in range(NT):
                b = r % 2
                ld = S.op("sync",
                          lambda h, b=b, r=r: h.dma_start(out=x_tm[b][:, :], in_=x_h[r * 128:(r + 1) * 128, :]),
                          waits=[act_toks[b]], inc="dX", incn=16)
                t1 = S.op("vector",
                          lambda h, b=b: h.reduce_sum(ncol[b][:, :], x_tm[b][:, :], axis=AX.X, negate=True),
                          waits=[ld], inc="sV")
                t2 = S.op("vector",
                          lambda h, b=b: h.tensor_single_scalar(nmean[b][:, :], ncol[b][:, :], 1.0 / D, ALU.mult),
                          waits=[t1], inc="sV")
                act_toks[b] = S.op("scalar",
                                   lambda h, b=b: h.activation(xibf[b][:, :], x_tm[b][:, :], AF.Identity,
                                                               bias=nmean[b][:, 0:1]),
                                   waits=[t2], inc="sA")
                for dc in range(8):
                    pb = dc % 2
                    tt = S.op("tensor",
                              lambda h, b=b, dc=dc, pb=pb: h.transpose(
                                  ptb[pb][:, :], xibf[b][:, dc * 128:(dc + 1) * 128], identb[:, :]),
                              waits=[act_toks[b], ptb_free[pb]], inc="sT")
                    ptb_free[pb] = S.op(
                        "vector",
                        lambda h, dc=dc, r=r, pb=pb: h.tensor_copy(
                            actfm[:, dc * T + r * 128: dc * T + (r + 1) * 128], ptb[pb][:, :]),
                        waits=[tt], inc="sV")
            p0_done = S.snap()

        sz_spill = {}
        u_done = {}
        with ExitStack() as p1:
            pec = p1.enter_context
            xz = [pec(nc.sbuf_tensor(f"xz{i}", [128, T + 4], BF16)) for i in range(2)]
            diag = [pec(nc.sbuf_tensor(f"diag{i}", [128, 4 * 128], BF16)) for i in range(2)]
            szt = [pec(nc.sbuf_tensor(f"szt{i}", [128, 512], BF16)) for i in range(2)]
            pz = [pec(nc.psum_tensor(f"pz{i}", [128, 512], F32)) for i in range(2)]
            pc = [pec(nc.psum_tensor(f"pc{i}", [128, 512], F32)) for i in range(2)]

            zpad = []
            for i in range(2):
                S.op("vector", lambda h, i=i: h.memset(xz[i][:, 0:1], 0.0), inc="sV")
                zpad.append(S.op("vector", lambda h, i=i: h.memset(xz[i][:, T + 1:T + 4], 0.0), inc="sV"))

            wz_tok = S.op("gpsimd",
                          lambda h: h.dma_start(out=wa[:, :].rearrange("p (k m) -> p k m", m=DI),
                                                in_=w_in_h[:, DI:2 * DI].rearrange("(k p) m -> p k m", p=128)),
                          inc="dW", incn=16)
            wxb_tok = S.op("gpsimd",
                           lambda h: h.dma_start(out=wb[:, :].rearrange("p (k m) -> p k m", m=DI),
                                                 in_=w_in_h[:, 0:DI].rearrange("(k p) m -> p k m", p=128)),
                           inc="dW", incn=16)

            pz_free = [None, None]
            pc_free = [None, None]
            szt_free = [None, None]

            for mz in range(16):
                for tc in range(NTC):
                    pb = (mz * NTC + tc) % 2
                    sb = pb
                    mt = S.op("tensor",
                              lambda h, mz=mz, tc=tc, pb=pb: [
                                  h.matmul(pz[pb][:, :], wa[:, k * DI + mz * 128:k * DI + (mz + 1) * 128],
                                           actfm[:, k * T + tc * 512:k * T + (tc + 1) * 512],
                                           start=(k == 0), stop=(k == 7)) for k in range(8)][-1],
                              waits=[wz_tok, pz_free[pb]] + (p0_done if mz == 0 and tc == 0 else []),
                              inc="sT")
                    sil = S.op("scalar",
                               lambda h, pb=pb, sb=sb: h.activation(szt[sb][:, :], pz[pb][:, :], AF.Silu),
                               waits=[mt, szt_free[sb]], inc="sA")
                    pz_free[pb] = sil
                    sz_spill[(mz, tc)] = S.op("scalar",
                                              lambda h, sb=sb, mz=mz, tc=tc: h.dma_start(
                                                  out=sz_d[mz * 128:(mz + 1) * 128, tc * 512:(tc + 1) * 512],
                                                  in_=szt[sb][:, :]),
                                              inc="dSp", incn=16)
                    szt_free[sb] = sz_spill[(mz, tc)]

            xz_free = [None, None]
            dg_free = [None, None]
            for mx in range(16):
                b = mx % 2
                ev = []
                for tc in range(NTC):
                    pb = tc % 2
                    mt = S.op("tensor",
                              lambda h, mx=mx, tc=tc, pb=pb: [
                                  h.matmul(pz[pb][:, :], wb[:, k * DI + mx * 128:k * DI + (mx + 1) * 128],
                                           actfm[:, k * T + tc * 512:k * T + (tc + 1) * 512],
                                           start=(k == 0), stop=(k == 7)) for k in range(8)][-1],
                              waits=[wxb_tok, pz_free[pb]] + ([xz_free[b]] if tc == 0 else []),
                              inc="sT")
                    e = S.op("vector",
                             lambda h, b=b, tc=tc, pb=pb: h.tensor_copy(
                                 xz[b][:, 1 + tc * 512:1 + (tc + 1) * 512], pz[pb][:, :]),
                             waits=[mt], inc="sV")
                    pz_free[pb] = e
                    ev.append(e)
                dg = None
                for k in range(4):
                    dg = S.op("vector",
                              lambda h, b=b, mx=mx, k=k: h.tensor_scalar(
                                  diag[b][:, k * 128:(k + 1) * 128], identb[:, :],
                                  cw_t[:, mx * 4 + k:mx * 4 + k + 1], None, ALU.mult),
                              waits=[dg_free[b]] if k == 0 else [], inc="sV")
                for tc in range(NTC):
                    pb = tc % 2
                    ct = S.op("tensor",
                              lambda h, b=b, tc=tc, pb=pb: [
                                  h.matmul(pc[pb][:, :], diag[b][:, k * 128:(k + 1) * 128],
                                           xz[b][:, tc * 512 + k: tc * 512 + k + 512],
                                           start=(k == 0), stop=(k == 3)) for k in range(4)][-1],
                              waits=[ev[-1], dg, zpad[b], pc_free[pb]], inc="sT")
                    if tc == NTC - 1:
                        xz_free[b] = ct
                        dg_free[b] = ct
                    u_done[(mx, tc)] = S.op("scalar",
                                            lambda h, mx=mx, tc=tc, pb=pb: h.activation(
                                                big[:, mx * T + tc * 512: mx * T + (tc + 1) * 512],
                                                pc[pb][:, :], AF.Silu, bias=cb_t[:, mx:mx + 1]),
                                            waits=[ct], inc="sA")
                    pc_free[pb] = u_done[(mx, tc)]
            m1_done = S.snap()
            if DEBUG:
                S.op("sync", lambda h: h.dma_start(
                    out=dbg["dbg_u"].rearrange("(k p) m -> p k m", p=128),
                    in_=big[:, 0:16 * T].rearrange("p (k m) -> p k m", m=T)),
                    waits=m1_done, inc="dX", incn=16)

        ddma = {}
        with ExitStack() as p2a:
            pec = p2a.enter_context
            wx_t = pec(nc.sbuf_tensor("wx_t", [128, 16 * 32], BF16))
            tmp1 = pec(nc.sbuf_tensor("tmp1", [128, 512], F32))
            tmp2 = pec(nc.sbuf_tensor("tmp2", [128, 512], F32))
            tmp3 = pec(nc.sbuf_tensor("tmp3", [128, 512], F32))
            dtt = [pec(nc.sbuf_tensor(f"dtt{i}", [128, 512], F32)) for i in range(2)]
            pd = [pec(nc.psum_tensor(f"pd{i}", [128, 512], F32)) for i in range(2)]
            pbcB = pec(nc.psum_tensor("pbcB", [16, 512], F32))
            pbcC = pec(nc.psum_tensor("pbcC", [16, 512], F32))

            wdtA = S.op("gpsimd",
                        lambda h: h.dma_start(out=wa[:, :].rearrange("p (k m) -> p k m", m=DI),
                                              in_=wdt_h[0:1024, :].rearrange("(k p) m -> p k m", p=128)),
                        waits=m1_done, inc="dW", incn=16)
            wdtB = S.op("gpsimd",
                        lambda h: h.dma_start(out=wb[:, :].rearrange("p (k m) -> p k m", m=DI),
                                              in_=wdt_h[1024:2048, :].rearrange("(k p) m -> p k m", p=128)),
                        inc="dW", incn=16)
            wxT = S.op("gpsimd",
                       lambda h: h.dma_start(out=wx_t[:, :].rearrange("p (k m) -> p k m", m=32),
                                             in_=wx_h[:, :].rearrange("(k p) m -> p k m", p=128)),
                       inc="dW", incn=16)

            bc_done = []
            pbcB_free = None
            pbcC_free = None
            for tc in range(NTC):
                mb = S.op("tensor",
                          lambda h, tc=tc: [h.matmul(pbcB[:, :], wx_t[:, k * 32:k * 32 + 16],
                                                     big[:, k * T + tc * 512:k * T + (tc + 1) * 512],
                                                     start=(k == 0), stop=(k == 15)) for k in range(16)][-1],
                          waits=[wxT, pbcB_free] + (m1_done if tc == 0 else []), inc="sT")
                eb = S.op("vector", lambda h, tc=tc: h.tensor_copy(braw[:, tc * 512:(tc + 1) * 512], pbcB[:, :]),
                          waits=[mb], inc="sV")
                pbcB_free = eb
                mc = S.op("tensor",
                          lambda h, tc=tc: [h.matmul(pbcC[:, :], wx_t[:, k * 32 + 16:k * 32 + 32],
                                                     big[:, k * T + tc * 512:k * T + (tc + 1) * 512],
                                                     start=(k == 0), stop=(k == 15)) for k in range(16)][-1],
                          waits=[pbcC_free], inc="sT")
                ecv = S.op("vector", lambda h, tc=tc: h.tensor_copy(craw[:, tc * 512:(tc + 1) * 512], pbcC[:, :]),
                           waits=[mc], inc="sV")
                pbcC_free = ecv
                bc_done += [eb, ecv]
            if DEBUG:
                S.op("sync", lambda h: h.dma_start(out=dbg["dbg_bc"][:, 0:T], in_=braw), waits=bc_done,
                     inc="dX", incn=16)
                S.op("sync", lambda h: h.dma_start(out=dbg["dbg_bc"][:, T:2 * T], in_=craw), waits=bc_done,
                     inc="dX", incn=16)

            pd_free = [None, None]
            dtt_free = [None, None]
            v1prev = [None]
            for md in range(16):
                for tc in range(NTC):
                    pb = (md * NTC + tc) % 2
                    mt = S.op("tensor",
                              lambda h, md=md, tc=tc, pb=pb: [
                                  h.matmul(pd[pb][:, :],
                                           (wa if k < 8 else wb)[:, (k % 8) * DI + md * 128:(k % 8) * DI + (md + 1) * 128],
                                           big[:, k * T + tc * 512:k * T + (tc + 1) * 512],
                                           start=(k == 0), stop=(k == 15)) for k in range(16)][-1],
                              waits=[wdtA, wdtB, pd_free[pb]] +
                                    ([bc_done[-1]] if md == 0 and tc == 0 else []),
                              inc="sT")
                    a1 = S.op("scalar",
                              lambda h, md=md, pb=pb: h.activation(tmp1[:, :], pd[pb][:, :], AF.Abs,
                                                                   bias=bdt_t[:, md:md + 1]),
                              waits=[mt, v1prev[0]], inc="sA")
                    a2 = S.op("scalar", lambda h: h.activation(tmp2[:, :], tmp1[:, :], AF.Exp, scale=-1.0),
                              inc="sA")
                    a3 = S.op("scalar", lambda h: h.activation(tmp1[:, :], tmp2[:, :], AF.Ln, bias=1.0),
                              inc="sA")
                    a4 = S.op("scalar",
                              lambda h, md=md, pb=pb: h.activation(tmp3[:, :], pd[pb][:, :], AF.Relu,
                                                                   bias=bdt_t[:, md:md + 1]),
                              inc="sA")
                    pd_free[pb] = a4
                    db = pb
                    v1 = S.op("vector",
                              lambda h, db=db: h.tensor_add(dtt[db][:, :], tmp1[:, :], tmp3[:, :]),
                              waits=[a4, dtt_free[db]], inc="sV")
                    v1prev[0] = v1
                    ddma[(md, tc)] = S.op("scalar",
                                          lambda h, md=md, tc=tc, db=db: h.dma_start(
                                              out=delta_d[md * 128:(md + 1) * 128, tc * 512:(tc + 1) * 512],
                                              in_=dtt[db][:, :]),
                                          waits=[v1], inc="dSp", incn=16)
                    dtt_free[db] = ddma[(md, tc)]
            m2a_done = S.snap()
            if DEBUG:
                S.op("sync", lambda h: h.dma_start(out=dbg["dbg_delta"][:, :], in_=delta_d[:, :]),
                     waits=m2a_done, inc="dX", incn=16)

        ybf_spill = {}
        with ExitStack() as p2b:
            pec = p2b.enter_context
            dtl = [pec(nc.sbuf_tensor(f"dtl{i}", [128, 512], F32)) for i in range(2)]
            dut = [pec(nc.sbuf_tensor(f"dut{i}", [128, 512], BF16)) for i in range(2)]
            dAt = [pec(nc.sbuf_tensor(f"dAt{i}", [128, 512], F32)) for i in range(2)]
            dBt = [pec(nc.sbuf_tensor(f"dBt{i}", [128, 512], BF16)) for i in range(2)]
            ht = [pec(nc.sbuf_tensor(f"ht{i}", [128, 512], BF16)) for i in range(2)]
            pt = [pec(nc.sbuf_tensor(f"pt{i}", [128, 512], BF16)) for i in range(2)]
            yac = [pec(nc.sbuf_tensor(f"yac{i}", [128, 512], F32)) for i in range(2)]
            szs = [pec(nc.sbuf_tensor(f"szs{i}", [128, 512], BF16)) for i in range(2)]
            ybt = [pec(nc.sbuf_tensor(f"ybt{i}", [128, 512], BF16)) for i in range(2)]
            pbb = [pec(nc.psum_tensor(f"pbb{i}", [128, 512], F32)) for i in range(2)]

            woutT = S.op("gpsimd",
                         lambda h: h.dma_start(out=wa[:, :].rearrange("p (k m) -> p k m", m=1024),
                                               in_=wout_h[:, :].rearrange("(k p) m -> p k m", p=128)),
                         waits=m2a_done, inc="dW", incn=16)

            dA_free = [None] * 2
            dB_free = [None] * 2
            h_free = [None] * 2
            p_free = [None] * 2
            ydma = [None, None]
            dt_free = [None, None]
            sz_free = [None, None]
            pbb_free = [None, None]
            for tc in range(NTC):
                bcast = []
                for s in range(16):
                    for (raw, dst, pbi) in ((braw, Bbc, 0), (craw, Cbc, 1)):
                        mt = S.op("tensor",
                                  lambda h, s=s, tc=tc, raw=raw, pbi=pbi: h.matmul(
                                      pbb[pbi][:, :], sel16[:, s * 128:(s + 1) * 128],
                                      raw[:, tc * 512:(tc + 1) * 512], start=True, stop=True),
                                  waits=[pbb_free[pbi]] + (m2a_done if tc == 0 and s == 0 else []),
                                  inc="sT")
                        e = S.op("vector",
                                 lambda h, s=s, dst=dst, pbi=pbi: h.tensor_copy(
                                     dst[:, s * 512:(s + 1) * 512], pbb[pbi][:, :]),
                                 waits=[mt], inc="sV")
                        pbb_free[pbi] = e
                        bcast.append(e)
                for mx in range(16):
                    db = mx % 2
                    dld = S.op("sync",
                               lambda h, mx=mx, tc=tc, db=db: h.dma_start(
                                   out=dtl[db][:, :],
                                   in_=delta_d[mx * 128:(mx + 1) * 128, tc * 512:(tc + 1) * 512]),
                               waits=[ddma[(mx, tc)], dt_free[db]], inc="dX", incn=16)
                    sld = S.op("sync",
                               lambda h, mx=mx, tc=tc, db=db: h.dma_start(
                                   out=szs[db][:, :],
                                   in_=sz_d[mx * 128:(mx + 1) * 128, tc * 512:(tc + 1) * 512]),
                               waits=[sz_spill[(mx, tc)], sz_free[db]], inc="dX", incn=16)
                    dtok = S.op("vector",
                                lambda h, mx=mx, tc=tc, db=db: h.tensor_tensor(
                                    dut[db][:, :], dtl[db][:, :],
                                    big[:, mx * T + tc * 512:mx * T + (tc + 1) * 512], ALU.mult),
                                waits=[dld], inc="sV")
                    ylast = None
                    for s in range(16):
                        j = s % 2
                        adA = S.op("scalar",
                                   lambda h, mx=mx, s=s, j=j, db=db: h.activation(
                                       dAt[j][:, :], dtl[db][:, :], AF.Exp,
                                       scale=a_t[:, mx * DS + s:mx * DS + s + 1]),
                                   waits=[dld, dA_free[j]], inc="sA")
                        S.op("vector",
                             lambda h, s=s, j=j, db=db: h.tensor_tensor(
                                 dBt[j][:, :], dut[db][:, :], Bbc[:, s * 512:(s + 1) * 512], ALU.mult),
                             waits=[dtok, dB_free[j], bcast[2 * s]], inc="sV")
                        hsc = S.op("vector",
                                   lambda h, mx=mx, s=s, j=j: h.tensor_tensor_scan(
                                       ht[j][:, :], dAt[j][:, :], dBt[j][:, :],
                                       hc[:, mx * DS + s:mx * DS + s + 1], ALU.mult, ALU.add),
                                   waits=[adA, h_free[j], tok_hc0], inc="sV")
                        dA_free[j] = hsc
                        dB_free[j] = hsc
                        S.op("vector",
                             lambda h, mx=mx, s=s, j=j: h.tensor_copy(
                                 hc[:, mx * DS + s:mx * DS + s + 1], ht[j][:, 511:512]),
                             waits=[hsc], inc="sV")
                        pm = S.op("vector",
                                  lambda h, s=s, j=j: h.tensor_tensor(
                                      pt[j][:, :], ht[j][:, :], Cbc[:, s * 512:(s + 1) * 512], ALU.mult),
                                  waits=[hsc, p_free[j], bcast[2 * s + 1]], inc="sV")
                        h_free[j] = pm
                        if s == 0:
                            ylast = S.op("gpsimd",
                                         lambda h, j=j: h.tensor_copy(yac[0][:, :], pt[j][:, :]),
                                         waits=[pm], inc="sP")
                        else:
                            src, dst = (yac[0], yac[1]) if s % 2 == 1 else (yac[1], yac[0])
                            ylast = S.op("gpsimd",
                                         lambda h, j=j, src=src, dst=dst: h.tensor_tensor(
                                             dst[:, :], src[:, :], pt[j][:, :], ALU.add),
                                         waits=[pm], inc="sP")
                        p_free[j] = ylast
                    y1 = S.op("vector",
                              lambda h, mx=mx, tc=tc: h.scalar_tensor_tensor(
                                  yac[1][:, :], big[:, mx * T + tc * 512:mx * T + (tc + 1) * 512],
                                  dv_t[:, mx:mx + 1], yac[1][:, :], ALU.mult, ALU.add),
                              waits=[ylast], inc="sV")
                    yo = mx % 2
                    y2 = S.op("gpsimd",
                              lambda h, yo=yo, db=db: h.tensor_tensor(
                                  ybt[yo][:, :], yac[1][:, :], szs[db][:, :], ALU.mult),
                              waits=[y1, sld, ydma[yo]], inc="sP")
                    sz_free[db] = y2
                    dt_free[db] = y2
                    ydma[yo] = S.op("scalar",
                                    lambda h, mx=mx, tc=tc, yo=yo: h.dma_start(
                                        out=ybf_d[mx * 128:(mx + 1) * 128, tc * 512:(tc + 1) * 512],
                                        in_=ybt[yo][:, :]),
                                    waits=[y2], inc="dSp", incn=16)
                    ybf_spill[(mx, tc)] = ydma[yo]
            m2b_done = S.snap()
            if DEBUG:
                S.op("sync", lambda h: h.dma_start(out=dbg["dbg_ybf"][:, :], in_=ybf_d[:, :]),
                     waits=m2b_done, inc="dX", incn=16)

        with ExitStack() as p3:
            pec = p3.enter_context
            otm = pec(nc.sbuf_tensor("otm", [128, D], F32))
            xrm = pec(nc.sbuf_tensor("xrm", [128, D], F32))
            x1tm = pec(nc.sbuf_tensor("x1tm", [128, D], F32))
            x1bf = pec(nc.sbuf_tensor("x1bf", [128, D], BF16))
            silu_t = [pec(nc.sbuf_tensor(f"silu{i}", [128, 512], BF16)) for i in range(2)]
            lg = pec(nc.sbuf_tensor("lg", [128, E], F32))
            sc_t = pec(nc.sbuf_tensor("sc_t", [128, E], F32))
            sc2_t = pec(nc.sbuf_tensor("sc2_t", [128, E], F32))
            mk = pec(nc.sbuf_tensor("mk", [128, E], F32))
            gtm = pec(nc.sbuf_tensor("gtm", [128, E], F32))
            c_ns = pec(nc.sbuf_tensor("c_ns", [128, 1], F32))
            c_mean = pec(nc.sbuf_tensor("c_mean", [128, 1], F32))
            c_sq = pec(nc.sbuf_tensor("c_sq", [128, 1], F32))
            c_var = pec(nc.sbuf_tensor("c_var", [128, 1], F32))
            c_rstd = pec(nc.sbuf_tensor("c_rstd", [128, 1], F32))
            ph1 = pec(nc.psum_tensor("ph1", [128, 512], F32))
            ph3 = pec(nc.psum_tensor("ph3", [128, 512], F32))
            py = [pec(nc.psum_tensor(f"py{i}", [128, 512], F32)) for i in range(2)]
            pf = [pec(nc.psum_tensor(f"pf{i}", [128, 512], F32)) for i in range(2)]
            pb_ = [pec(nc.psum_tensor(f"pb_{i}", [128, 128], BF16)) for i in range(2)]

            L = {}

            ffc = []
            ffc.append(cl(gwt_t, gwt_h[:, :], waits=m2b_done))
            ffc.append(cl(b1_t, b1t_h[:, :]))
            ffc.append(cl(b3_t, b3t_h[:, :]))
            ffc.append(cl(bs1_t, bs1_h[:, :]))
            ffc.append(cl(bs3_t, bs3_h[:, :]))
            ffc.append(cl(bs2_t, bs2_h[:, :]))
            ffc.append(cl(b2_t, b2_h[:, :]))
            ffc.append(cl(sel8, sel8_h[:, :]))
            ffc_done = ffc[-1]

            gb1 = S.op("tensor", lambda h: h.matmul(pf[0][:, 0:E], ones1[:, :], gbrow[:, :],
                                                    start=True, stop=True),
                       waits=m2b_done, inc="sT")
            gb2 = S.op("vector", lambda h: h.tensor_copy(gbbc[:, :], pf[0][:, 0:E]),
                       waits=[gb1], inc="sV")
            L[("pf", 0)] = gb2

            x1_spill = {}
            ln1_done = {}

            def do_m3_half(hh, wout_tok):
                for tcl in range(HTC):
                    tc = hh * HTC + tcl
                    ytile = []
                    for k in range(16):
                        ytile.append(S.op("sync",
                                          lambda h, k=k, tc=tc: h.dma_start(
                                              out=ybs[:, k * 512:(k + 1) * 512],
                                              in_=ybf_d[k * 128:(k + 1) * 128, tc * 512:(tc + 1) * 512]),
                                          waits=[ybf_spill[(k, tc)], L.get(("ybs", k)), L.get("he_mm")] +
                                                (m2b_done if hh == 0 and tcl == 0 and k == 0 else []),
                                          inc="dX", incn=16))
                    ofm_tok = {}
                    for mo in range(8):
                        pbk = mo % 2
                        mt = S.op("tensor",
                                  lambda h, mo=mo, pbk=pbk: [
                                      h.matmul(py[pbk][:, :], wa[:, k * 1024 + mo * 128:k * 1024 + (mo + 1) * 128],
                                               ybs[:, k * 512:(k + 1) * 512],
                                               start=(k == 0), stop=(k == 15)) for k in range(16)][-1],
                                  waits=[wout_tok, L.get(("py", pbk))] + ytile, inc="sT")
                        L["wa_mm"] = mt
                        ofm_tok[mo] = S.op("vector",
                                           lambda h, mo=mo, pbk=pbk: h.tensor_copy(
                                               ofm[:, mo * 512:(mo + 1) * 512], py[pbk][:, :]),
                                           waits=[mt, L.get(("ofm-rd", mo)), L.get("he_mm")], inc="sV")
                        L[("py", pbk)] = ofm_tok[mo]
                    for k in range(16):
                        L[("ybs", k)] = ofm_tok[7]
                    for rl in range(4):
                        r = tc * 4 + rl
                        xr = S.op("sync",
                                  lambda h, r=r: h.dma_start(out=xrm[:, :], in_=x_h[r * 128:(r + 1) * 128, :]),
                                  waits=[L.get("xrm")], inc="dX", incn=16)
                        ot = None
                        for dc in range(8):
                            fb = dc % 2
                            tt = S.op("tensor",
                                      lambda h, dc=dc, rl=rl, fb=fb: h.transpose(
                                          pf[fb][:, 0:128],
                                          ofm[:, dc * 512 + rl * 128: dc * 512 + (rl + 1) * 128],
                                          identf[:, :]),
                                      waits=[ofm_tok[dc], L.get(("pf", fb))], inc="sT")
                            L[("ofm-rd", dc)] = tt
                            L["ofm-last"] = tt
                            ot = S.op("vector",
                                      lambda h, dc=dc, fb=fb: h.scalar_tensor_tensor(
                                          otm[:, dc * 128:(dc + 1) * 128],
                                          xrm[:, dc * 128:(dc + 1) * 128],
                                          2.0, pf[fb][:, 0:128], ALU.mult, ALU.add),
                                      waits=[tt, xr, L.get("otm")], inc="sV")
                            L[("pf", fb)] = ot
                        L["xrm"] = ot
                        s1 = S.op("vector",
                                  lambda h: h.reduce_sum(c_ns[:, :], otm[:, :], axis=AX.X, negate=True),
                                  waits=[ot], inc="sV")
                        s2 = S.op("vector",
                                  lambda h: h.tensor_single_scalar(c_mean[:, :], c_ns[:, :], -1.0 / D, ALU.mult),
                                  waits=[s1], inc="sV")
                        s3 = S.op("scalar",
                                  lambda h: h.activation(x1tm[:, :], otm[:, :], AF.Square,
                                                         accum_out=c_sq[:, 0:1]),
                                  waits=[ot, L.get("x1tm"), L.get("x1sp"), L.get("outdma")], inc="sA")
                        s4 = S.op("scalar", lambda h: h.activation(c_var[:, :], c_mean[:, :], AF.Square),
                                  waits=[s2], inc="sA")
                        s5 = S.op("vector",
                                  lambda h: h.tensor_scalar(c_ns[:, :], c_sq[:, :], 1.0 / D,
                                                            c_var[:, 0:1], ALU.mult, ALU.subtract),
                                  waits=[s3, s4], inc="sV")
                        s6 = S.op("scalar", lambda h: h.activation(c_rstd[:, :], c_ns[:, :], AF.Ln, bias=fc[:, 0:1]),
                                  waits=[s5], inc="sA")
                        s7 = S.op("scalar", lambda h: h.activation(c_rstd[:, :], c_rstd[:, :], AF.Exp, scale=fc[:, 1:2]),
                                  inc="sA")
                        ap = S.op("vector",
                                  lambda h: h.tensor_scalar(x1tm[:, :], otm[:, :],
                                                            c_mean[:, 0:1], c_rstd[:, 0:1],
                                                            ALU.subtract, ALU.mult),
                                  waits=[s7, s3, s2], inc="sV")
                        L["otm"] = ap
                        x1_spill[r] = S.op("sync",
                                           lambda h, r=r: h.dma_start(out=x1_d[r * 128:(r + 1) * 128, :],
                                                                      in_=x1tm[:, :]),
                                           waits=[ap], inc="dX", incn=16)
                        L["x1sp"] = x1_spill[r]
                        cbt = S.op("vector", lambda h: h.tensor_copy(x1bf[:, :], x1tm[:, :]),
                                   waits=[ap, L.get("x1bf")], inc="sV")
                        L["x1tm"] = cbt
                        if DEBUG:
                            S.op("sync", lambda h, r=r: h.dma_start(
                                out=dbg["dbg_x1"][r * 128:(r + 1) * 128, :], in_=x1tm[:, :]),
                                waits=[cbt], inc="dX", incn=16)
                        lastev = None
                        for dc in range(8):
                            bb = dc % 2
                            tt = S.op("tensor",
                                      lambda h, dc=dc, bb=bb: h.transpose(
                                          pb_[bb][:, :], x1bf[:, dc * 128:(dc + 1) * 128], identb[:, :]),
                                      waits=[cbt, L.get(("pb", bb))], inc="sT")
                            lastev = S.op("vector",
                                          lambda h, dc=dc, r=r, bb=bb: h.tensor_copy(
                                              actfm[:, dc * T + r * 128:dc * T + (r + 1) * 128], pb_[bb][:, :]),
                                          waits=[tt], inc="sV")
                            L[("pb", bb)] = lastev
                        L["x1bf"] = lastev
                        ln1_done[r] = lastev

            def do_ffn_half(hh):
                base_t = hh * HALF
                gate_toks = []
                for rl in range(NTH):
                    r = hh * NTH + rl
                    mt = S.op("tensor",
                              lambda h, r=r: [
                                  h.matmul(pf[0][:, 0:E], actfm[:, k * T + r * 128:k * T + (r + 1) * 128],
                                           gwt_t[:, k * E:(k + 1) * E],
                                           start=(k == 0), stop=(k == 7)) for k in range(8)][-1],
                              waits=[ln1_done[r], gb2, ffc_done, L.get(("pf", 0))], inc="sT")
                    g1 = S.op("vector", lambda h: h.tensor_add(lg[:, :], pf[0][:, 0:E], gbbc[:, :]),
                              waits=[mt, L.get("lg")], inc="sV")
                    L[("pf", 0)] = g1
                    g2 = S.op("vector", lambda h: h.reduce_max(c_ns[:, :], lg[:, :], axis=AX.X),
                              waits=[g1], inc="sV")
                    g2b = S.op("vector", lambda h: h.tensor_single_scalar(c_mean[:, :], c_ns[:, :], -1.0, ALU.mult),
                               waits=[g2], inc="sV")
                    g3 = S.op("scalar", lambda h: h.activation(sc_t[:, :], lg[:, :], AF.Exp, bias=c_mean[:, 0:1]),
                              waits=[g2b, L.get("sc_t")], inc="sA")
                    g4 = S.op("vector", lambda h: h.reduce_sum(c_sq[:, :], sc_t[:, :], axis=AX.X),
                              waits=[g3, g2b], inc="sV")
                    g5 = S.op("vector", lambda h: h.reciprocal(c_var[:, :], c_sq[:, :]), waits=[g4], inc="sV")
                    g6 = S.op("vector", lambda h: h.tensor_single_scalar(sc_t[:, :], sc_t[:, :],
                                                                         c_var[:, 0:1], ALU.mult), waits=[g5], inc="sV")
                    g7 = S.op("vector", lambda h: h.reduce_max(c_ns[:, :], sc_t[:, :], axis=AX.X),
                              waits=[g6], inc="sV")
                    g8 = S.op("vector", lambda h: h.tensor_single_scalar(mk[:, :], sc_t[:, :], c_ns[:, 0:1],
                                                                         ALU.is_ge), waits=[g7], inc="sV")
                    g9 = S.op("vector", lambda h: h.scalar_tensor_tensor(sc2_t[:, :], mk[:, :], -1e30, sc_t[:, :],
                                                                         ALU.mult, ALU.add), waits=[g8], inc="sV")
                    g10 = S.op("vector", lambda h: h.reduce_max(c_ns[:, :], sc2_t[:, :], axis=AX.X),
                               waits=[g9], inc="sV")
                    g11 = S.op("vector", lambda h: h.tensor_single_scalar(mk[:, :], sc_t[:, :], c_ns[:, 0:1],
                                                                          ALU.is_ge), waits=[g10], inc="sV")
                    g12 = S.op("vector", lambda h: h.tensor_mul(gtm[:, :], sc_t[:, :], mk[:, :]),
                               waits=[g11, L.get("gdump")], inc="sV")
                    L["lg"] = g12
                    L["sc_t"] = g12
                    if DEBUG:
                        L["gdump"] = S.op("sync", lambda h, r=r: h.dma_start(
                            out=dbg["dbg_g"][r * 128:(r + 1) * 128, :], in_=gtm[:, :]),
                            waits=[g12], inc="dX", incn=16)
                    tg = S.op("tensor", lambda h: h.transpose(pf[1][0:8, 0:128], gtm[:, :], identf[:, :]),
                              waits=[g12, L.get(("pf", 1))], inc="sT")
                    gev = S.op("vector",
                               lambda h, rl=rl: h.tensor_copy(gT[0:8, rl * 128:(rl + 1) * 128],
                                                              pf[1][0:8, 0:128]),
                               waits=[tg, L.get("gtm"), ffc_done], inc="sV")
                    L[("pf", 1)] = gev
                    L["gtm"] = gev
                    gate_toks.append(gev)
                gate_done = gate_toks[-1]

                py0_free = L.get(("py", 0))
                for tcl in range(HTC):
                    for mo in range(8):
                        mt = S.op("tensor",
                                  lambda h, tcl=tcl, mo=mo: h.matmul(
                                      py[0][:, :], b2_t[:, mo * 128:(mo + 1) * 128],
                                      gT[:, tcl * 512:(tcl + 1) * 512], start=True, stop=True),
                                  waits=[gate_done, py0_free], inc="sT")
                        py0_free = S.op("vector",
                                        lambda h, tcl=tcl, mo=mo: h.tensor_scalar(
                                            ysb[:, mo * HALF + tcl * 512: mo * HALF + (tcl + 1) * 512],
                                            py[0][:, :], bs2_t[:, mo:mo + 1], None, ALU.add),
                                        waits=[mt, L.get(("ysb", tcl, mo)), L.get("wa_mm")], inc="sV")
                        L[("ysb", tcl, mo)] = py0_free
                L[("py", 0)] = py0_free

                experts = []
                for e in range(E):
                    experts.append(dict(
                        w1=w1t_h[e * D:(e + 1) * D, :], w3=w3t_h[e * D:(e + 1) * D, :],
                        w2=w2t_h[e * IN:(e + 1) * IN, :],
                        b1c=[b1_t[:, e * 8 + mi:e * 8 + mi + 1] for mi in range(8)],
                        b3c=[b3_t[:, e * 8 + mi:e * 8 + mi + 1] for mi in range(8)],
                        gated=e))
                experts.append(dict(
                    w1=ws1t_h[:, 0:1024], w3=ws3t_h[:, 0:1024], w2=ws2t_h[0:1024, :],
                    b1c=[bs1_t[:, mi:mi + 1] for mi in range(8)],
                    b3c=[bs3_t[:, mi:mi + 1] for mi in range(8)], gated=None))
                experts.append(dict(
                    w1=ws1t_h[:, 1024:2048], w3=ws3t_h[:, 1024:2048], w2=ws2t_h[1024:2048, :],
                    b1c=[bs1_t[:, 8 + mi:8 + mi + 1] for mi in range(8)],
                    b3c=[bs3_t[:, 8 + mi:8 + mi + 1] for mi in range(8)], gated=None))

                pending = {}

                def load_one(j, dram_ap, wait_toks):
                    sl = j
                    tok = S.op("gpsimd",
                               lambda h, sl=sl, dram_ap=dram_ap: h.dma_start(
                                   out=big[:, sl * 8192: sl * 8192 + 8192].rearrange("p (k m) -> p k m", m=1024),
                                   in_=dram_ap.rearrange("(k p) m -> p k m", p=128)),
                               waits=wait_toks, inc="dW", incn=16)
                    return sl, tok

                first_w = list(m2b_done) if hh == 0 else [L.get("he_mm"), L.get("ph1"), L.get("ph3")]
                pending[0] = tuple(zip(*[load_one(j, experts[0][k], first_w)
                                         for j, k in enumerate(("w1", "w3", "w2"))]))

                for ie, ex in enumerate(experts):
                    sls, tks = pending.pop(ie)
                    (s1l, s3l, s2l), (t1, t3, t2) = sls, tks
                    gated = ex["gated"]
                    gtok = None
                    if gated is not None:
                        for tcl in range(HTC):
                            gm = S.op("tensor",
                                      lambda h, tcl=tcl, gated=gated: h.matmul(
                                          pf[0][:, :], sel8[:, gated * 128:(gated + 1) * 128],
                                          gT[:, tcl * 512:(tcl + 1) * 512], start=True, stop=True),
                                      waits=[gate_done, ffc_done, L.get(("pf", 0))], inc="sT")
                            gtok = S.op("vector",
                                        lambda h, tcl=tcl: h.tensor_copy(
                                            Gt[:, tcl * 512:(tcl + 1) * 512], pf[0][:, :]),
                                        waits=[gm, L.get("Gt")], inc="sV")
                            L[("pf", 0)] = gtok
                    he_free = L.get("he_mm")
                    ofm_guard = L.get("ofm-last")
                    hlast = {}
                    for tcl in range(HTC):
                        tgl = base_t + tcl * 512
                        for mi in range(8):
                            m1_ = S.op("tensor",
                                       lambda h, s1l=s1l, mi=mi, tgl=tgl: [
                                           h.matmul(ph1[:, :],
                                                    big[:, s1l * 8192 + k * 1024 + mi * 128:
                                                         s1l * 8192 + k * 1024 + (mi + 1) * 128],
                                                    actfm[:, k * T + tgl:k * T + tgl + 512],
                                                    start=(k == 0), stop=(k == 7)) for k in range(8)][-1],
                                       waits=[t1, L.get("ph1")], inc="sT")
                            m3_ = S.op("tensor",
                                       lambda h, s3l=s3l, mi=mi, tgl=tgl: [
                                           h.matmul(ph3[:, :],
                                                    big[:, s3l * 8192 + k * 1024 + mi * 128:
                                                         s3l * 8192 + k * 1024 + (mi + 1) * 128],
                                                    actfm[:, k * T + tgl:k * T + tgl + 512],
                                                    start=(k == 0), stop=(k == 7)) for k in range(8)][-1],
                                       waits=[t3, L.get("ph3")], inc="sT")
                            sb = mi % 2
                            sa = S.op("scalar",
                                      lambda h, mi=mi, sb=sb, ex=ex: h.activation(
                                          silu_t[sb][:, :], ph1[:, :], AF.Silu, bias=ex["b1c"][mi]),
                                      waits=[m1_, L.get(("silu", sb))], inc="sA")
                            L["ph1"] = sa
                            hv = S.op("vector",
                                      lambda h, mi=mi, ex=ex, sb=sb: h.scalar_tensor_tensor(
                                          he[:, mi * 512:(mi + 1) * 512], ph3[:, :], ex["b3c"][mi],
                                          silu_t[sb][:, :], ALU.add, ALU.mult),
                                      waits=[m3_, sa, he_free, ofm_guard], inc="sV")
                            L["ph3"] = hv
                            L[("silu", sb)] = hv
                            if gated is not None:
                                hv = S.op("vector",
                                          lambda h, mi=mi, tcl=tcl: h.tensor_tensor(
                                              he[:, mi * 512:(mi + 1) * 512], he[:, mi * 512:(mi + 1) * 512],
                                              Gt[:, tcl * 512:(tcl + 1) * 512], ALU.mult),
                                          waits=[gtok], inc="sV")
                            hlast[mi] = hv
                        wmts = []
                        for mo in range(8):
                            pbk = mo % 2
                            mt = S.op("tensor",
                                      lambda h, s2l=s2l, mo=mo, pbk=pbk: [
                                          h.matmul(py[pbk][:, :],
                                                   big[:, s2l * 8192 + k * 1024 + mo * 128:
                                                        s2l * 8192 + k * 1024 + (mo + 1) * 128],
                                                   he[:, k * 512:(k + 1) * 512],
                                                   start=(k == 0), stop=(k == 7)) for k in range(8)][-1],
                                      waits=[t2, L.get(("py", pbk))] + [hlast[k] for k in range(8)],
                                      inc="sT")
                            wmts.append(mt)
                            yt = S.op("vector",
                                      lambda h, mo=mo, tcl=tcl, pbk=pbk: h.tensor_tensor(
                                          ysb[:, mo * HALF + tcl * 512: mo * HALF + (tcl + 1) * 512],
                                          ysb[:, mo * HALF + tcl * 512: mo * HALF + (tcl + 1) * 512],
                                          py[pbk][:, :], ALU.add),
                                      waits=[mt, L.get(("ysb", tcl, mo))], inc="sV")
                            L[("py", pbk)] = yt
                            L[("ysb", tcl, mo)] = yt
                        he_free = wmts[-1]
                        L["he_mm"] = wmts[-1]
                    if gated is not None:
                        L["Gt"] = L["he_mm"]
                    if ie + 1 < len(experts):
                        nx = experts[ie + 1]
                        _, n1 = load_one(0, nx["w1"], [L["ph1"]])
                        _, n3 = load_one(1, nx["w3"], [L["ph3"]])
                        _, n2 = load_one(2, nx["w2"], [L["he_mm"]])
                        pending[ie + 1] = ((0, 1, 2), (n1, n3, n2))

                ln2_rd = []
                for rl in range(NTH):
                    r = hh * NTH + rl
                    tclr = rl // 4
                    xr = S.op("sync",
                              lambda h, r=r: h.dma_start(out=xrm[:, :], in_=x1_d[r * 128:(r + 1) * 128, :]),
                              waits=[x1_spill[r], L.get("xrm")], inc="dX", incn=16)
                    ot = None
                    for dc in range(8):
                        fb = dc % 2
                        tt = S.op("tensor",
                                  lambda h, dc=dc, rl=rl, fb=fb: h.transpose(
                                      pf[fb][:, 0:128],
                                      ysb[:, dc * HALF + rl * 128: dc * HALF + (rl + 1) * 128],
                                      identf[:, :]),
                                  waits=[L.get(("ysb", tclr, dc)), L.get(("pf", fb))], inc="sT")
                        L[("ysb", tclr, dc)] = tt
                        ln2_rd.append(tt)
                        ot = S.op("vector",
                                  lambda h, dc=dc, fb=fb: h.scalar_tensor_tensor(
                                      otm[:, dc * 128:(dc + 1) * 128], xrm[:, dc * 128:(dc + 1) * 128],
                                      1.0, pf[fb][:, 0:128], ALU.mult, ALU.add),
                                  waits=[tt, xr, L.get("otm")], inc="sV")
                        L[("pf", fb)] = ot
                    L["xrm"] = ot
                    s1 = S.op("vector", lambda h: h.reduce_sum(c_ns[:, :], otm[:, :], axis=AX.X, negate=True),
                              waits=[ot], inc="sV")
                    s2 = S.op("vector", lambda h: h.tensor_single_scalar(c_mean[:, :], c_ns[:, :], -1.0 / D, ALU.mult),
                              waits=[s1], inc="sV")
                    s3 = S.op("scalar", lambda h: h.activation(x1tm[:, :], otm[:, :], AF.Square,
                                                               accum_out=c_sq[:, 0:1]),
                              waits=[ot, L.get("x1tm"), L.get("outdma"), L.get("x1sp")], inc="sA")
                    s4 = S.op("scalar", lambda h: h.activation(c_var[:, :], c_mean[:, :], AF.Square),
                              waits=[s2], inc="sA")
                    s5 = S.op("vector", lambda h: h.tensor_scalar(c_ns[:, :], c_sq[:, :], 1.0 / D,
                                                                  c_var[:, 0:1], ALU.mult, ALU.subtract),
                              waits=[s3, s4], inc="sV")
                    s6 = S.op("scalar", lambda h: h.activation(c_rstd[:, :], c_ns[:, :], AF.Ln, bias=fc[:, 0:1]),
                              waits=[s5], inc="sA")
                    s7 = S.op("scalar", lambda h: h.activation(c_rstd[:, :], c_rstd[:, :], AF.Exp, scale=fc[:, 1:2]),
                              inc="sA")
                    ap = S.op("vector", lambda h: h.tensor_scalar(x1tm[:, :], otm[:, :], c_mean[:, 0:1],
                                                                  c_rstd[:, 0:1], ALU.subtract, ALU.mult),
                              waits=[s7, s3, s2], inc="sV")
                    L["otm"] = ap
                    L["x1tm"] = ap
                    L["outdma"] = S.op("sync",
                                       lambda h, r=r: h.dma_start(out=out_h[r * 128:(r + 1) * 128, :],
                                                                  in_=x1tm[:, :]),
                                       waits=[ap], inc="dO", incn=16)
                return ln2_rd

            do_m3_half(0, woutT)
            rd0 = do_ffn_half(0)
            woutT2 = S.op("gpsimd",
                          lambda h: h.dma_start(out=wa[:, :].rearrange("p (k m) -> p k m", m=1024),
                                                in_=wout_h[:, :].rearrange("(k p) m -> p k m", p=128)),
                          waits=rd0[-16:], inc="dW", incn=16)
            do_m3_half(1, woutT2)
            do_ffn_half(1)

            S.op("sync", lambda h: h.nop(), waits=S.group_waits("dO"))

        with nc.Block() as block:
            S.emit(block)

    return nc


def _host_layout(ins):
    f = np.float32
    bb = ml_dtypes.bfloat16

    def tile_col(v, nk):
        v = np.asarray(v, f).reshape(nk, 128, -1).transpose(1, 0, 2)
        return np.ascontiguousarray(v.reshape(128, -1))

    w = {}
    w["w_in"] = np.ascontiguousarray(np.asarray(ins["L0_attn_W_in"], f))
    w["cw"] = tile_col(np.asarray(ins["L0_attn_conv_w"], f).reshape(4, DI).T, 16)
    w["cb"] = tile_col(np.asarray(ins["L0_attn_conv_b"], f).reshape(DI, 1), 16)
    w["w_x"] = np.ascontiguousarray(np.asarray(ins["L0_attn_W_x"], f))
    w["w_dt"] = np.ascontiguousarray(np.asarray(ins["L0_attn_W_dt"], f))
    w["b_dt"] = tile_col(np.asarray(ins["L0_attn_b_dt"], f).reshape(DI, 1), 16)
    w["w_out"] = np.ascontiguousarray(np.asarray(ins["L0_attn_W_out"], f))
    w["a_mat"] = tile_col(np.asarray(ins["L0_attn_A"], f), 16)
    w["d_vec"] = tile_col(np.asarray(ins["L0_attn_D"], f).reshape(DI, 1), 16)
    w["gate_wt"] = tile_col(np.ascontiguousarray(np.asarray(ins["L0_ffn_gate_w"], f).T), 8).astype(bb)
    w["gate_b"] = np.asarray(ins["L0_ffn_gate_b"], f).reshape(1, E)
    w["w1t"] = np.ascontiguousarray(np.asarray(ins["L0_ffn_w1"], f).transpose(0, 2, 1).reshape(E * D, IN))
    w["w3t"] = np.ascontiguousarray(np.asarray(ins["L0_ffn_w3"], f).transpose(0, 2, 1).reshape(E * D, IN))
    w["w2t"] = np.ascontiguousarray(np.asarray(ins["L0_ffn_w2"], f).transpose(0, 2, 1).reshape(E * IN, D))
    b1 = np.asarray(ins["L0_ffn_b1"], f)
    b3 = np.asarray(ins["L0_ffn_b3"], f)
    bt1 = np.zeros((128, 8 * E), f)
    bt3 = np.zeros((128, 8 * E), f)
    for e in range(E):
        for mi in range(8):
            bt1[:, e * 8 + mi] = b1[e, mi * 128:(mi + 1) * 128]
            bt3[:, e * 8 + mi] = b3[e, mi * 128:(mi + 1) * 128]
    w["b1t"] = bt1
    w["b3t"] = bt3
    w["b2"] = np.ascontiguousarray(np.asarray(ins["L0_ffn_b2"], f))
    w["ws1t"] = np.ascontiguousarray(np.asarray(ins["L0_ffn_ws1"], f).T)
    w["bs1"] = tile_col(np.asarray(ins["L0_ffn_bs1"], f).reshape(SH, 1), 16)
    w["ws3t"] = np.ascontiguousarray(np.asarray(ins["L0_ffn_ws3"], f).T)
    w["bs3"] = tile_col(np.asarray(ins["L0_ffn_bs3"], f).reshape(SH, 1), 16)
    w["ws2t"] = np.ascontiguousarray(np.asarray(ins["L0_ffn_ws2"], f).T)
    w["bs2"] = tile_col(np.asarray(ins["L0_ffn_bs2"], f).reshape(D, 1), 8)
    sel16 = np.zeros((16, 16 * 128), bb)
    for s in range(16):
        sel16[s, s * 128:(s + 1) * 128] = 1
    w["sel16"] = sel16
    sel8 = np.zeros((8, 8 * 128), f)
    for s in range(8):
        sel8[s, s * 128:(s + 1) * 128] = 1
    w["sel8"] = sel8
    w["identb"] = np.eye(128, dtype=bb)
    w["identf"] = np.eye(128, dtype=f)
    w["ones1"] = np.ones((1, 128), f)
    fcv = np.zeros((128, 2), f)
    fcv[:, 0] = 1e-5
    fcv[:, 1] = -0.5
    w["fc"] = fcv
    return w


def kernel(**ins):
    T = ins["L0_attn_inp"].shape[1]
    w = _host_layout(ins)
    x = np.asarray(ins["L0_attn_inp"], np.float32)
    in_maps = []
    for b in range(NCORES):
        m = dict(w)
        m["x"] = np.ascontiguousarray(x[b])
        in_maps.append(m)
    nc = build_program(T)
    res = run_bass_kernel_spmd(nc, in_maps, core_ids=list(range(NCORES)))
    return np.stack([np.asarray(r["out"]) for r in res.results], axis=0)
